# version 04
# baseline (speedup 1.0000x reference)
import numpy as np
from contextlib import ExitStack

import concourse.bass as bass
import concourse.mybir as mybir
from concourse.bass_utils import run_bass_kernel_spmd
from concourse.alu_op_type import AluOpType as OP

F32 = mybir.dt.float32
F32R = mybir.dt.float32r
BF16 = mybir.dt.bfloat16
AF = mybir.ActivationFunctionType
AX = mybir.AxisListType

D = 1024
H = 4096
DC = D // 128
HC = H // 128
SCALE = 1.0 / 32.0
LN_EPS = 1e-5
Q_EPS = 1e-6


def build(S=8192, dbg=False):
    NT = S // 128
    NG_A = NT // 4
    NG_B = NT // 2
    nc = bass.Bass("TRN2", target_bir_lowering=False, debug=False,
                   detect_race_conditions=False)

    x_d = nc.dram_tensor("x", [S, D], F32, kind="ExternalInput").ap()
    xT_d = nc.dram_tensor("xT", [D, S], F32, kind="ExternalInput").ap()
    wq_d = nc.dram_tensor("wq", [D, D], F32, kind="ExternalInput").ap()
    wk_d = nc.dram_tensor("wk", [D, D], F32, kind="ExternalInput").ap()
    wvT_d = nc.dram_tensor("wvT", [D, D], F32, kind="ExternalInput").ap()
    wr_d = nc.dram_tensor("wr", [D, D], F32, kind="ExternalInput").ap()
    wfcT_d = nc.dram_tensor("wfcT", [D, H], F32, kind="ExternalInput").ap()
    wprojT_d = nc.dram_tensor("wprojT", [H, D], F32, kind="ExternalInput").ap()
    lnw_row_d = nc.dram_tensor("lnw_row", [1, H], F32, kind="ExternalInput").ap()
    lnwT_d = nc.dram_tensor("lnwT", [128, HC], F32, kind="ExternalInput").ap()
    lnbT_d = nc.dram_tensor("lnbT", [128, HC], F32, kind="ExternalInput").ap()
    out_d = nc.dram_tensor("out", [S, D], F32, kind="ExternalOutput").ap()

    KI = "ExternalOutput" if dbg else "Internal"
    k_s = nc.dram_tensor("k_s", [S, D], F32, kind=KI).ap()
    xtp_s = nc.dram_tensor("xtp_s", [NT, 128, 2048], BF16, kind=KI).ap()
    x1_s = nc.dram_tensor("x1_s", [S, D], F32, kind=KI).ap()
    x1h_s = nc.dram_tensor("x1h_s", [S, D], BF16, kind=KI).ap()
    x1l_s = nc.dram_tensor("x1l_s", [S, D], BF16, kind=KI).ap()
    gh_s = nc.dram_tensor("gh_s", [S, H], BF16, kind=KI).ap()
    gl_s = nc.dram_tensor("gl_s", [S, H], BF16, kind=KI).ap()
    wfcq_s = nc.dram_tensor("wfcq_s", [DC, 128, H], BF16, kind=KI).ap()
    wprojq_s = nc.dram_tensor("wprojq_s", [HC, 128, D], BF16, kind=KI).ap()
    if dbg:
        dbg_d = nc.dram_tensor("dbg", [4, D], F32, kind="ExternalOutput").ap()
        dbg2_d = nc.dram_tensor("dbg2", [128, 4 * NT], F32,
                                kind="ExternalOutput").ap()

    xT_r = xT_d.rearrange("(c p) s -> p c s", p=128)

    es = ExitStack()
    with es:
        sem = {}
        for name in [
            "bar",
            "p_ld0", "p_ld1", "p_cons", "p_gr", "p_bc", "p_qt",
            "p_qc0", "p_qc1", "p_mm", "p_lnld", "p_lnp", "p_spl",
            "a_ldx0", "a_ldx1", "a_split", "a_qk", "a_exp", "a_w2", "a_gq",
            "a_kev", "a_pst0", "a_pst1", "a_kst0", "a_kst1", "a_gqev",
            "b_ldk0", "b_ldk1", "b_p", "b_e", "b_w2", "b_gk", "b_gkt",
            "b_wld0", "b_wld1", "b_wsplit", "b_wc", "b_wsum", "b_gqb",
            "c_ldp0", "c_ldp1", "c_ldx0", "c_ldx1", "c_a", "c_y", "c_s2",
            "c_v", "c_sq", "c_x1", "c_st0", "c_st1",
            "d_wfc", "d_wfcG", "d_lnw", "d_ldT0", "d_ldT1", "d_ldT20",
            "d_ldT21", "d_hd", "d_gelu", "d_gp", "d_gst", "d_acc",
            "e_wpj", "e_wpjG", "e_ldT0", "e_ldT1", "e_ldT20", "e_ldT21",
            "e_ldx0", "e_ldx1", "e_r2", "e_mm", "e_o1",
            "e_y2", "e_s23", "e_v3", "e_sq3", "e_out", "e_st0", "e_st1",
            "e_std", "e_c12b",
        ]:
            sem[name] = es.enter_context(nc.semaphore(name))

        BAR = sem["bar"]
        bar_k = [0]

        def ph_start(eng):
            if bar_k[0] > 0:
                eng.wait_ge(BAR, 5 * bar_k[0])

        def ph_end(eng):
            eng.drain().then_inc(BAR, 1)

        def W(eng, s, v):
            if v > 0:
                eng.wait_ge(sem[s], v)

        ones_r = es.enter_context(nc.sbuf_tensor("ones_r", [128, 2], F32R))
        ones_f = es.enter_context(nc.sbuf_tensor("ones_f", [128, 2], F32))
        gq_sb = es.enter_context(nc.sbuf_tensor("gq_sb", [1, D], F32))
        c12_sb = es.enter_context(nc.sbuf_tensor("c12_sb", [2, D], F32))
        s1_t = es.enter_context(nc.sbuf_tensor("s1_t", [128, NT], F32))
        s2_t = es.enter_context(nc.sbuf_tensor("s2_t", [128, NT], F32))

        with ExitStack() as st0:
            wq_hi = st0.enter_context(nc.sbuf_tensor("wq_hi", [128, D * DC], BF16))
            wq_lo = st0.enter_context(nc.sbuf_tensor("wq_lo", [128, D * DC], BF16))
            wk_hi = st0.enter_context(nc.sbuf_tensor("wk_hi", [128, D * DC], BF16))
            wk_lo = st0.enter_context(nc.sbuf_tensor("wk_lo", [128, D * DC], BF16))
            wtile = [st0.enter_context(nc.sbuf_tensor(f"wtile{i}", [128, H], F32))
                     for i in range(2)]
            qtile = [st0.enter_context(nc.sbuf_tensor(f"qtile{i}", [128, H], BF16))
                     for i in range(2)]
            gcol = st0.enter_context(nc.sbuf_tensor("gcol", [128, 40], F32))
            gred = st0.enter_context(nc.sbuf_tensor("gred", [128, 1], F32))
            gtot = st0.enter_context(nc.sbuf_tensor("gtot", [1, 128], F32))
            thrT = st0.enter_context(nc.sbuf_tensor("thrT", [1, 4], F32))
            thr_c = st0.enter_context(nc.sbuf_tensor("thr_c", [128, 2], F32))
            lnpair = st0.enter_context(nc.sbuf_tensor("lnpair", [128, 4 * HC], BF16))
            lnT_f = st0.enter_context(nc.sbuf_tensor("lnT_f", [128, 2 * HC], F32))

            def tile_src(t):
                if t < 8:
                    return wfcT_d[(t % 8) * 128:(t % 8) * 128 + 128, :], H
                if t < 16:
                    return wfcT_d[(t % 8) * 128:(t % 8) * 128 + 128, :], H
                if t < 48:
                    i = t - 16
                    return wprojT_d[i * 128:(i + 1) * 128, :], D
                if t < 80:
                    i = t - 48
                    return wprojT_d[i * 128:(i + 1) * 128, :], D
                if t < 88:
                    i = t - 80
                    return wq_d[i * 128:(i + 1) * 128, :], D
                i = t - 88
                return wk_d[i * 128:(i + 1) * 128, :], D

            with (
                nc.psum_tensor("p0_c12", [2, D], F32) as p0_c12,
                nc.Block("p0") as blk,
            ):
                def p0_sync(sync):
                    ph_start(sync)
                    sync.dma_start(out=lnT_f[:, 0:HC], in_=lnwT_d[:, :])\
                        .then_inc(sem["p_lnld"], 16)
                    sync.dma_start(out=lnT_f[:, HC:2 * HC], in_=lnbT_d[:, :])\
                        .then_inc(sem["p_lnld"], 16)
                    for t in range(96):
                        if t < 82:
                            W(sync, "p_cons", t - 1)
                        else:
                            W(sync, "p_spl", t - 81)
                        src, w = tile_src(t)
                        sync.dma_start(out=wtile[t % 2][:, 0:w], in_=src)\
                            .then_inc(sem[f"p_ld{t % 2}"], 16)
                    ph_end(sync)

                def quant_ops(vector, t, w, qk):
                    W(vector, f"p_qc{qk % 2}", 16 * (qk // 2))
                    if qk - 2 >= 8:
                        W(vector, "p_mm", qk - 9)
                    vector.tensor_scalar(qtile[t % 2][:, 0:w],
                                         wtile[t % 2][:, 0:w],
                                         thr_c[:, 0:1], None, OP.is_gt)
                    vector.scalar_tensor_tensor(
                        qtile[t % 2][:, 0:w], wtile[t % 2][:, 0:w],
                        thr_c[:, 1:2], qtile[t % 2][:, 0:w],
                        OP.is_lt, OP.subtract)\
                        .then_inc(sem["p_cons"], 1)
                    vector.tensor_scalar(qtile[t % 2][:, 0:w],
                                         qtile[t % 2][:, 0:w],
                                         -1.0, None, OP.mult)\
                        .then_inc(sem["p_qt"], 1)

                def p0_vector(vector):
                    ph_start(vector)
                    vector.memset(ones_f[:], 1.0)
                    vector.tensor_copy(ones_r[:], ones_f[:])
                    for i in range(8):
                        W(vector, f"p_ld{i % 2}", 16 * (i // 2 + 1))
                        vector.reduce_sum(gcol[:, i:i + 1], wtile[i % 2][:],
                                          axis=AX.X, apply_absolute_value=True)\
                            .then_inc(sem["p_cons"], 1)
                    vector.drain()
                    vector.reduce_sum(gred[:], gcol[:, 0:8], axis=AX.X)\
                        .then_inc(sem["p_gr"], 1)
                    W(vector, "p_bc", 16)
                    vector.reduce_sum(thrT[:, 0:1], gtot[:], axis=AX.X)
                    vector.drain()
                    vector.tensor_scalar(thrT[:, 0:1], thrT[:, 0:1],
                                         1.0 / (D * H), None, OP.mult)
                    vector.drain()
                    vector.tensor_scalar(thrT[:, 0:1], thrT[:, 0:1], Q_EPS, 0.5,
                                         OP.add, OP.mult)
                    vector.drain().then_inc(sem["p_gr"], 1)
                    W(vector, "p_bc", 16 * 9)
                    vector.tensor_scalar(thr_c[:, 1:2], thr_c[:, 0:1], -1.0,
                                         None, OP.mult)
                    vector.drain()
                    for i in range(8):
                        t = 8 + i
                        W(vector, f"p_ld{t % 2}", 16 * (t // 2 + 1))
                        quant_ops(vector, t, H, i)
                    for i in range(32):
                        t = 16 + i
                        W(vector, f"p_ld{t % 2}", 16 * (t // 2 + 1))
                        vector.reduce_sum(gcol[:, 8 + i:9 + i],
                                          wtile[t % 2][:, 0:D],
                                          axis=AX.X, apply_absolute_value=True)\
                            .then_inc(sem["p_cons"], 1)
                    vector.drain()
                    vector.reduce_sum(gred[:], gcol[:, 8:40], axis=AX.X)\
                        .then_inc(sem["p_gr"], 1)
                    W(vector, "p_bc", 16 * 10)
                    vector.reduce_sum(thrT[:, 2:3], gtot[:], axis=AX.X)
                    vector.drain()
                    vector.tensor_scalar(thrT[:, 2:3], thrT[:, 2:3],
                                         1.0 / (D * H), None, OP.mult)
                    vector.drain()
                    vector.tensor_scalar(thrT[:, 2:3], thrT[:, 2:3], Q_EPS, 0.5,
                                         OP.add, OP.mult)
                    vector.drain().then_inc(sem["p_gr"], 1)
                    W(vector, "p_bc", 16 * 18)
                    vector.tensor_scalar(thr_c[:, 1:2], thr_c[:, 0:1], -1.0,
                                         None, OP.mult)
                    vector.drain()
                    W(vector, "p_lnld", 32)
                    pv = lnpair[:].rearrange("p (h c s) -> p h c s", h=2, s=2)
                    vector.tensor_copy(pv[:, 0, :, 0], lnT_f[:, 0:HC])
                    vector.tensor_copy(pv[:, 0, :, 1], lnT_f[:, HC:2 * HC])
                    vector.tensor_sub(pv[:, 1, :, 0], lnT_f[:, 0:HC],
                                      pv[:, 0, :, 0])
                    vector.tensor_sub(pv[:, 1, :, 1], lnT_f[:, HC:2 * HC],
                                      pv[:, 0, :, 1])
                    vector.drain().then_inc(sem["p_lnp"], 1)
                    for i in range(32):
                        t = 48 + i
                        W(vector, f"p_ld{t % 2}", 16 * (t // 2 + 1))
                        quant_ops(vector, t, D, 8 + i)
                    for i in range(16):
                        t = 80 + i
                        W(vector, f"p_ld{t % 2}", 16 * (t // 2 + 1))
                        hi = wq_hi if i < 8 else wk_hi
                        lo = wq_lo if i < 8 else wk_lo
                        c = (i % 8) * D
                        vector.tensor_copy(hi[:, c:c + D], wtile[t % 2][:, 0:D])
                        vector.tensor_sub(lo[:, c:c + D], wtile[t % 2][:, 0:D],
                                          hi[:, c:c + D])\
                            .then_inc(sem["p_spl"], 1)
                    W(vector, "p_mm", 32)
                    vector.tensor_copy(c12_sb[:], p0_c12[:])
                    ph_end(vector)

                def p0_tensor(tensor):
                    ph_start(tensor)
                    W(tensor, "p_lnp", 1)
                    for i in range(32):
                        W(tensor, "p_qt", 8 + i + 1)
                        t = 48 + i
                        inst = None
                        for half in range(2):
                            c = (half * HC + i) * 2
                            for nch in range(2):
                                cs = slice(nch * 512, nch * 512 + 512)
                                inst = tensor.matmul(
                                    p0_c12[:, cs], lnpair[:, c:c + 2],
                                    qtile[t % 2][:, 0:D][:, cs],
                                    start=(i == 0 and half == 0),
                                    stop=(i == 31 and half == 1))
                        inst.then_inc(sem["p_mm"], 1)
                    ph_end(tensor)

                def p0_scalar(scalar):
                    ph_start(scalar)
                    ph_end(scalar)

                def p0_gpsimd(gpsimd):
                    ph_start(gpsimd)

                    def tree(dst, srcrow, base):
                        gpsimd.dma_start(out=dst[0:1, 0:1], in_=srcrow)\
                            .then_inc(sem["p_bc"], 16)
                        for k in range(6):
                            w = 1 << k
                            W(gpsimd, "p_bc", base + 16 * (k + 1))
                            gpsimd.dma_start(out=dst[w:2 * w, 0:1],
                                             in_=dst[0:w, 0:1])\
                                .then_inc(sem["p_bc"], 16)
                        W(gpsimd, "p_bc", base + 16 * 7)
                        gpsimd.dma_start(out=dst[64:128, 0:1],
                                         in_=dst[0:64, 0:1])\
                            .then_inc(sem["p_bc"], 16)

                    W(gpsimd, "p_gr", 1)
                    gpsimd.dma_start(out=gtot[0:1, :], in_=gred[:, 0:1])\
                        .then_inc(sem["p_bc"], 16)
                    W(gpsimd, "p_gr", 2)
                    tree(thr_c, thrT[0:1, 0:1], 16)
                    for i in range(8):
                        W(gpsimd, "p_qt", i + 1)
                        gpsimd.dma_start(out=wfcq_s[i],
                                         in_=qtile[i % 2][:])\
                            .then_inc(sem[f"p_qc{i % 2}"], 16)
                    W(gpsimd, "p_gr", 3)
                    gpsimd.dma_start(out=gtot[0:1, :], in_=gred[:, 0:1])\
                        .then_inc(sem["p_bc"], 16)
                    W(gpsimd, "p_gr", 4)
                    tree(thr_c, thrT[0:1, 2:3], 160)
                    for i in range(32):
                        W(gpsimd, "p_qt", 8 + i + 1)
                        gpsimd.dma_start(out=wprojq_s[i],
                                         in_=qtile[i % 2][:, 0:D])\
                            .then_inc(sem[f"p_qc{i % 2}"], 16)
                    W(gpsimd, "p_qc0", 16 * 20)
                    W(gpsimd, "p_qc1", 16 * 20)
                    ph_end(gpsimd)

                blk.sync(p0_sync)
                blk.vector(p0_vector)
                blk.tensor(p0_tensor)
                blk.scalar(p0_scalar)
                blk.gpsimd(p0_gpsimd)
            bar_k[0] += 1

            xtile = [st0.enter_context(nc.sbuf_tensor(f"xtile{i}", [128, D], F32))
                     for i in range(2)]
            pairb = [st0.enter_context(nc.sbuf_tensor(f"pairb{i}", [128, 2048], BF16))
                     for i in range(2)]
            ebuf = [st0.enter_context(nc.sbuf_tensor(f"ebuf{i}", [128, D], F32))
                    for i in range(2)]
            dcol = st0.enter_context(nc.sbuf_tensor("dcol", [128, 2], F32))
            rdcol = st0.enter_context(nc.sbuf_tensor("rdcol", [128, 2], F32))
            w2b = [st0.enter_context(nc.sbuf_tensor(f"w2b{i}", [128, D], F32R))
                   for i in range(2)]
            kev = [st0.enter_context(nc.sbuf_tensor(f"kev{i}", [128, D], F32))
                   for i in range(2)]

            with (
                nc.psum_tensor("pa_q0", [128, D], F32) as pa_q0,
                nc.psum_tensor("pa_q1", [128, D], F32) as pa_q1,
                nc.psum_tensor("pa_k", [128, D], F32) as pa_k,
                nc.psum_tensor("pa_gq", [1, D], F32) as pa_gq,
                nc.Block("pa") as blk,
            ):
                qps = [pa_q0, pa_q1]
                passes = [(0, 0), (1, 0), (0, 1)]

                def pa_sync(sync):
                    ph_start(sync)
                    for j in range(NT):
                        W(sync, "a_split", 2 * (j - 1))
                        sync.dma_start(
                            out=xtile[j % 2][:].rearrange("p (c t) -> p c t", c=DC),
                            in_=xT_r[:, :, j * 128:(j + 1) * 128],
                        ).then_inc(sem[f"a_ldx{j % 2}"], 16)
                    ph_end(sync)

                def pa_vector(vector):
                    ph_start(vector)
                    for jj in range(NT + 1):
                        if jj < NT:
                            j = jj
                            W(vector, f"a_ldx{j % 2}", 16 * (j // 2 + 1))
                            W(vector, "a_qk", 2 * (j - 1))
                            W(vector, f"a_pst{j % 2}", 16 * (j // 2))
                            dst = pairb[j % 2][:].rearrange(
                                "p (c h t) -> p c h t", c=DC, h=2)
                            src = xtile[j % 2][:].rearrange(
                                "p (c t) -> p c t", c=DC)
                            vector.tensor_copy(dst[:, :, 0, :], src)
                            vector.tensor_tensor(dst[:, :, 1, :], src,
                                                 dst[:, :, 0, :], OP.subtract)\
                                .then_inc(sem["a_split"], 2)
                        if jj >= 1:
                            i = jj - 1
                            p = i % 2
                            W(vector, "a_exp", i + 1)
                            vector.reciprocal(rdcol[:, p:p + 1], dcol[:, p:p + 1])
                            vector.drain()
                            W(vector, "a_gq", i - 1)
                            vector.scalar_tensor_tensor(
                                w2b[p][:], qps[p][:], rdcol[:, p:p + 1],
                                ebuf[p][:], OP.mult, OP.mult)\
                                .then_inc(sem["a_w2"], 1)
                            W(vector, "a_qk", 2 * i + 2)
                            W(vector, f"a_kst{i % 2}", 16 * (i // 2))
                            vector.tensor_copy(kev[p][:], pa_k[:])\
                                .then_inc(sem["a_kev"], 1)
                    W(vector, "a_gq", NT)
                    vector.tensor_copy(gq_sb[:], pa_gq[:])
                    vector.drain().then_inc(sem["a_gqev"], 1)
                    ph_end(vector)

                def pa_scalar(scalar):
                    ph_start(scalar)
                    for j in range(NT):
                        p = j % 2
                        W(scalar, "a_qk", 2 * j + 1)
                        W(scalar, "a_w2", j - 1)
                        scalar.activation(ebuf[p][:], qps[p][:], AF.Exp,
                                          scale=SCALE,
                                          accum_out=dcol[:, p:p + 1])
                        scalar.drain().then_inc(sem["a_exp"], 1)
                    ph_end(scalar)

                def _mm_block(tensor, psum, pb, whi, wlo, j0):
                    inst = None
                    for ip, (lh, rh) in enumerate(passes):
                        w = [whi, wlo][rh]
                        for kc in range(DC):
                            lhsT = pb[:, kc * 256 + lh * 128:
                                      kc * 256 + lh * 128 + 128]
                            for nch in range(2):
                                inst = tensor.matmul(
                                    psum[:, nch * 512:nch * 512 + 512],
                                    lhsT,
                                    w[:, kc * D + nch * 512:
                                      kc * D + nch * 512 + 512],
                                    start=(ip == 0 and kc == 0),
                                    stop=(ip == 2 and kc == DC - 1))
                    return inst

                def _pa_gq_mm(tensor, j):
                    W(tensor, "a_w2", j + 1)
                    rhs = w2b[j % 2][:]
                    inst = None
                    for nch in range(2):
                        inst = tensor.matmul(
                            pa_gq[:, nch * 512:nch * 512 + 512],
                            ones_r[:, 0:1], rhs[:, nch * 512:nch * 512 + 512],
                            start=(j == 0), stop=(j == NT - 1))
                    inst.then_inc(sem["a_gq"], 1)

                def pa_tensor(tensor):
                    ph_start(tensor)
                    for j in range(NT):
                        W(tensor, "a_split", 2 * (j + 1))
                        W(tensor, "a_w2", j - 1)
                        pb = pairb[j % 2][:]
                        _mm_block(tensor, qps[j % 2][:], pb, wq_hi, wq_lo, j)\
                            .then_inc(sem["a_qk"], 1)
                        W(tensor, "a_kev", j)
                        _mm_block(tensor, pa_k[:], pb, wk_hi, wk_lo, j)\
                            .then_inc(sem["a_qk"], 1)
                        if j >= 1:
                            _pa_gq_mm(tensor, j - 1)
                    _pa_gq_mm(tensor, NT - 1)
                    ph_end(tensor)

                def pa_gpsimd(gpsimd):
                    ph_start(gpsimd)
                    for j in range(NT):
                        W(gpsimd, "a_split", 2 * (j + 1))
                        gpsimd.dma_start(out=xtp_s[j], in_=pairb[j % 2][:])\
                            .then_inc(sem[f"a_pst{j % 2}"], 16)
                        W(gpsimd, "a_kev", j + 1)
                        gpsimd.dma_start(out=k_s[j * 128:(j + 1) * 128, :],
                                         in_=kev[j % 2][:])\
                            .then_inc(sem[f"a_kst{j % 2}"], 16)
                    W(gpsimd, "a_pst0", 16 * (NT // 2))
                    W(gpsimd, "a_pst1", 16 * (NT // 2))
                    W(gpsimd, "a_kst0", 16 * (NT // 2))
                    W(gpsimd, "a_kst1", 16 * (NT // 2))
                    ph_end(gpsimd)

                blk.sync(pa_sync)
                blk.vector(pa_vector)
                blk.scalar(pa_scalar)
                blk.tensor(pa_tensor)
                blk.gpsimd(pa_gpsimd)
            bar_k[0] += 1

        with ExitStack() as stw:
            wsum_hi = stw.enter_context(
                nc.sbuf_tensor("wsum_hi", [128, D * DC], BF16))
            wsum_lo = stw.enter_context(
                nc.sbuf_tensor("wsum_lo", [128, D * DC], BF16))

            with ExitStack() as st1:
                kbuf = [st1.enter_context(nc.sbuf_tensor(f"kbuf{i}", [128, D], F32))
                        for i in range(2)]
                pbuf = [st1.enter_context(nc.sbuf_tensor(f"pbuf{i}", [128, D], F32))
                        for i in range(2)]
                ebuf2 = [st1.enter_context(nc.sbuf_tensor(f"ebuf2{i}", [128, D], F32))
                         for i in range(2)]
                w2b2 = [st1.enter_context(nc.sbuf_tensor(f"w2b2{i}", [128, D], F32R))
                        for i in range(2)]
                dcol2 = st1.enter_context(nc.sbuf_tensor("dcol2", [128, 2], F32))
                rdcol2 = st1.enter_context(nc.sbuf_tensor("rdcol2", [128, 2], F32))
                gq_b = st1.enter_context(nc.sbuf_tensor("gq_b", [128, D], F32))
                gkt_sb = st1.enter_context(nc.sbuf_tensor("gkt_sb", [128, DC], F32))
                wvts_hi = st1.enter_context(
                    nc.sbuf_tensor("wvts_hi", [128, D * DC], BF16))
                wvts_lo = st1.enter_context(
                    nc.sbuf_tensor("wvts_lo", [128, D * DC], BF16))
                wr_hi = st1.enter_context(
                    nc.sbuf_tensor("wr_hi", [128, D * DC], BF16))
                wr_lo = st1.enter_context(
                    nc.sbuf_tensor("wr_lo", [128, D * DC], BF16))
                wraw = [st1.enter_context(
                    nc.sbuf_tensor(f"wraw{i}", [128, D], F32)) for i in range(2)]
                wsf = st1.enter_context(nc.sbuf_tensor("wsf", [128, D], F32))

                with (
                    nc.psum_tensor("pb_gkt", [128, 2 * DC], F32) as pb_gkt,
                    nc.psum_tensor("pb_wc0", [128, D], F32) as pb_wc0,
                    nc.psum_tensor("pb_wc1", [128, D], F32) as pb_wc1,
                    nc.Block("pb") as blk,
                ):
                    wcps = [pb_wc0, pb_wc1]

                    def pb_sync(sync):
                        ph_start(sync)
                        for j in range(NT):
                            W(sync, "b_p", j - 1)
                            sync.dma_start(
                                out=kbuf[j % 2][:],
                                in_=k_s[j * 128:(j + 1) * 128, :],
                            ).then_inc(sem[f"b_ldk{j % 2}"], 16)
                        for i in range(2 * DC):
                            W(sync, "b_wsplit", 2 * (i - 1))
                            src = wvT_d if i < DC else wr_d
                            r = (i % DC) * 128
                            sync.dma_start(out=wraw[i % 2][:],
                                           in_=src[r:r + 128, :])\
                                .then_inc(sem[f"b_wld{i % 2}"], 16)
                        for dc in range(DC):
                            W(sync, "b_wsplit", 4 * DC)
                            W(sync, "b_wsum", dc - 1)
                            sync.dma_start(out=wraw[dc % 2][:],
                                           in_=wq_d[dc * 128:(dc + 1) * 128, :])\
                                .then_inc(sem[f"b_wld{dc % 2}"], 16)
                        ph_end(sync)

                    def pb_vector(vector):
                        ph_start(vector)
                        W(vector, "b_gqb", 16 * 8)
                        for jj in range(NT + 1):
                            if jj < NT:
                                j = jj
                                W(vector, f"b_ldk{j % 2}", 16 * (j // 2 + 1))
                                W(vector, "b_e", j - 1)
                                vector.tensor_tensor(pbuf[j % 2][:],
                                                     kbuf[j % 2][:],
                                                     gq_b[:], OP.mult)\
                                    .then_inc(sem["b_p"], 1)
                            if jj >= 1:
                                i = jj - 1
                                p = i % 2
                                W(vector, "b_e", i + 1)
                                vector.reciprocal(rdcol2[:, p:p + 1],
                                                  dcol2[:, p:p + 1])
                                vector.drain()
                                W(vector, "b_gk", i - 1)
                                vector.scalar_tensor_tensor(
                                    w2b2[p][:], pbuf[p][:],
                                    rdcol2[:, p:p + 1], ebuf2[p][:],
                                    OP.mult, OP.mult)\
                                    .then_inc(sem["b_w2"], 1)
                        W(vector, "b_gk", NT)
                        vector.tensor_copy(
                            gkt_sb[:],
                            pb_gkt[:].rearrange("p (c two) -> p c two",
                                                two=2)[:, :, 0:1])
                        vector.drain().then_inc(sem["b_gkt"], 1)
                        for i in range(2 * DC):
                            W(vector, f"b_wld{i % 2}", 16 * (i // 2 + 1))
                            c = (i % DC) * D
                            if i < DC:
                                vector.tensor_scalar(
                                    wraw[i % 2][:], wraw[i % 2][:],
                                    gkt_sb[:, i:i + 1], None, OP.mult)
                                vector.tensor_copy(wvts_hi[:, c:c + D],
                                                   wraw[i % 2][:])
                                vector.tensor_sub(wvts_lo[:, c:c + D],
                                                  wraw[i % 2][:],
                                                  wvts_hi[:, c:c + D])\
                                    .then_inc(sem["b_wsplit"], 2)
                            else:
                                vector.tensor_copy(wr_hi[:, c:c + D],
                                                   wraw[i % 2][:])
                                vector.tensor_sub(wr_lo[:, c:c + D],
                                                  wraw[i % 2][:],
                                                  wr_hi[:, c:c + D])\
                                    .then_inc(sem["b_wsplit"], 2)
                        for dc in range(DC):
                            W(vector, "b_wc", dc + 1)
                            W(vector, f"b_wld{dc % 2}",
                              16 * ((2 * DC + dc) // 2 + 1))
                            vector.tensor_tensor(wsf[:], wcps[dc % 2][:],
                                                 wraw[dc % 2][:], OP.add)
                            c = dc * D
                            vector.tensor_copy(wsum_hi[:, c:c + D], wsf[:])
                            vector.tensor_sub(wsum_lo[:, c:c + D], wsf[:],
                                              wsum_hi[:, c:c + D])\
                                .then_inc(sem["b_wsum"], 1)
                        ph_end(vector)

                    def pb_scalar(scalar):
                        ph_start(scalar)
                        for j in range(NT):
                            p = j % 2
                            W(scalar, "b_p", j + 1)
                            W(scalar, "b_w2", j - 1)
                            scalar.activation(
                                ebuf2[p][:], pbuf[p][:], AF.Exp, scale=SCALE,
                                accum_out=dcol2[:, p:p + 1])
                            scalar.drain().then_inc(sem["b_e"], 1)
                        ph_end(scalar)

                    def pb_tensor(tensor):
                        ph_start(tensor)
                        for j in range(NT):
                            W(tensor, "b_w2", j + 1)
                            rhs = w2b2[j % 2][:]
                            inst = None
                            for c in range(DC):
                                inst = tensor.matmul(
                                    pb_gkt[:, 2 * c:2 * c + 2],
                                    rhs[:, c * 128:(c + 1) * 128],
                                    ones_r[:, 0:2],
                                    start=(j == 0 and c == 0),
                                    stop=(j == NT - 1 and c == DC - 1))
                            inst.then_inc(sem["b_gk"], 1)
                        for dc in range(DC):
                            W(tensor, "b_wsplit", 4 * DC)
                            W(tensor, "b_wsum", dc - 1)
                            inst = None
                            for ip, (lh, rh) in enumerate(
                                    [(0, 0), (1, 0), (0, 1)]):
                                lhs_t = [wvts_hi, wvts_lo][lh]
                                rhs_t = [wr_hi, wr_lo][rh]
                                for ec in range(DC):
                                    lhsT = lhs_t[:, ec * D + dc * 128:
                                                 ec * D + dc * 128 + 128]
                                    for nch in range(2):
                                        inst = tensor.matmul(
                                            wcps[dc % 2][:, nch * 512:
                                                         nch * 512 + 512],
                                            lhsT,
                                            rhs_t[:, ec * D + nch * 512:
                                                  ec * D + nch * 512 + 512],
                                            start=(ip == 0 and ec == 0),
                                            stop=(ip == 2 and ec == DC - 1))
                            inst.then_inc(sem["b_wc"], 1)
                        ph_end(tensor)

                    def pb_gpsimd(gpsimd):
                        ph_start(gpsimd)
                        W(gpsimd, "a_gqev", 1)
                        gpsimd.dma_start(out=gq_b[0:1, :], in_=gq_sb[:])\
                            .then_inc(sem["b_gqb"], 16)
                        for k in range(6):
                            w = 1 << k
                            W(gpsimd, "b_gqb", 16 * (k + 1))
                            gpsimd.dma_start(out=gq_b[w:2 * w, :],
                                             in_=gq_b[0:w, :])\
                                .then_inc(sem["b_gqb"], 16)
                        W(gpsimd, "b_gqb", 16 * 7)
                        gpsimd.dma_start(out=gq_b[64:128, :], in_=gq_b[0:64, :])\
                            .then_inc(sem["b_gqb"], 16)
                        W(gpsimd, "b_gqb", 16 * 8)
                        ph_end(gpsimd)

                    blk.sync(pb_sync)
                    blk.vector(pb_vector)
                    blk.scalar(pb_scalar)
                    blk.tensor(pb_tensor)
                    blk.gpsimd(pb_gpsimd)
                bar_k[0] += 1

            with ExitStack() as st2:
                xtpb = [st2.enter_context(
                    nc.sbuf_tensor(f"xtpb{i}", [128, 2048], BF16))
                    for i in range(2)]
                xbuf = [st2.enter_context(
                    nc.sbuf_tensor(f"xbuf{i}", [128, D], F32)) for i in range(2)]
                ybuf = [st2.enter_context(
                    nc.sbuf_tensor(f"ybuf{i}", [128, D], F32)) for i in range(2)]
                x1b = [st2.enter_context(
                    nc.sbuf_tensor(f"x1b{i}", [128, D], F32)) for i in range(2)]
                x1hb = [st2.enter_context(
                    nc.sbuf_tensor(f"x1hb{i}", [128, D], BF16)) for i in range(2)]
                x1lb = [st2.enter_context(
                    nc.sbuf_tensor(f"x1lb{i}", [128, D], BF16)) for i in range(2)]
                s1a = st2.enter_context(nc.sbuf_tensor("s1a", [128, NT], F32))
                s2a = st2.enter_context(nc.sbuf_tensor("s2a", [128, NT], F32))
                mucol = st2.enter_context(nc.sbuf_tensor("mucol", [128, 2], F32))
                vcol = st2.enter_context(nc.sbuf_tensor("vcol", [128, 2], F32))
                s0col = st2.enter_context(nc.sbuf_tensor("s0col", [128, 2], F32))
                tcol = st2.enter_context(nc.sbuf_tensor("tcol", [128, 2], F32))
                rcol = st2.enter_context(nc.sbuf_tensor("rcol", [128, 2], F32))
                nmcol = st2.enter_context(nc.sbuf_tensor("nmcol", [128, 2], F32))
                junk = st2.enter_context(nc.sbuf_tensor("junk1", [128, D], BF16))

                with (
                    nc.psum_tensor("c1_a0", [128, D], F32) as c1_a0,
                    nc.psum_tensor("c1_a1", [128, D], F32) as c1_a1,
                    nc.Block("pc1") as blk,
                ):
                    aps = [c1_a0, c1_a1]
                    passes = [(0, 0), (1, 0), (0, 1)]

                    def c1_sync(sync):
                        ph_start(sync)
                        for j in range(NT):
                            W(sync, "c_a", j - 1)
                            sync.dma_start(out=xtpb[j % 2][:], in_=xtp_s[j])\
                                .then_inc(sem[f"c_ldp{j % 2}"], 16)
                            W(sync, "c_y", j - 1)
                            sync.dma_start(out=xbuf[j % 2][:],
                                           in_=x_d[j * 128:(j + 1) * 128, :])\
                                .then_inc(sem[f"c_ldx{j % 2}"], 16)
                        ph_end(sync)

                    def c1_tensor(tensor):
                        ph_start(tensor)
                        for j in range(NT):
                            W(tensor, f"c_ldp{j % 2}", 16 * (j // 2 + 1))
                            W(tensor, "c_y", j - 1)
                            pb = xtpb[j % 2][:]
                            inst = None
                            for ip, (lh, rh) in enumerate(passes):
                                w = [wsum_hi, wsum_lo][rh]
                                for kc in range(DC):
                                    lhsT = pb[:, kc * 256 + lh * 128:
                                              kc * 256 + lh * 128 + 128]
                                    for nch in range(2):
                                        inst = tensor.matmul(
                                            aps[j % 2][:, nch * 512:
                                                       nch * 512 + 512],
                                            lhsT,
                                            w[:, kc * D + nch * 512:
                                              kc * D + nch * 512 + 512],
                                            start=(ip == 0 and kc == 0),
                                            stop=(ip == 2 and kc == DC - 1))
                            inst.then_inc(sem["c_a"], 1)
                        ph_end(tensor)

                    def c1_vector(vector):
                        ph_start(vector)
                        for jj in range(NT + 1):
                            if jj < NT:
                                j = jj
                                p = j % 2
                                W(vector, "c_a", j + 1)
                                W(vector, f"c_ldx{j % 2}", 16 * (j // 2 + 1))
                                W(vector, "c_s2", j - 1)
                                vector.scalar_tensor_tensor(
                                    ybuf[p][:], aps[p][:], 0.0, xbuf[p][:],
                                    OP.bypass, OP.add,
                                    accum_out=s1a[:, j:j + 1])\
                                    .then_inc(sem["c_y"], 1)
                            if jj >= 1:
                                i = jj - 1
                                p = i % 2
                                W(vector, "c_s2", i + 1)
                                vector.tensor_scalar(mucol[:, p:p + 1],
                                                     s1a[:, i:i + 1], 1.0 / D,
                                                     None, OP.mult)
                                vector.drain()
                                vector.tensor_tensor(tcol[:, p:p + 1],
                                                     mucol[:, p:p + 1],
                                                     mucol[:, p:p + 1], OP.mult)
                                vector.drain()
                                vector.scalar_tensor_tensor(
                                    vcol[:, p:p + 1], s2a[:, i:i + 1], 1.0 / D,
                                    tcol[:, p:p + 1], OP.mult, OP.subtract)
                                vector.drain()
                                vector.tensor_scalar(vcol[:, p:p + 1],
                                                     vcol[:, p:p + 1], LN_EPS,
                                                     None, OP.add)
                                vector.drain().then_inc(sem["c_v"], 1)
                                W(vector, "c_sq", i + 1)
                                vector.reciprocal(rcol[:, p:p + 1],
                                                  s0col[:, p:p + 1])
                                vector.drain()
                                vector.tensor_tensor(tcol[:, p:p + 1],
                                                     rcol[:, p:p + 1],
                                                     rcol[:, p:p + 1], OP.mult)
                                vector.drain()
                                vector.tensor_tensor(tcol[:, p:p + 1],
                                                     tcol[:, p:p + 1],
                                                     vcol[:, p:p + 1], OP.mult)
                                vector.drain()
                                vector.tensor_scalar(tcol[:, p:p + 1],
                                                     tcol[:, p:p + 1],
                                                     -0.5, 1.5, OP.mult, OP.add)
                                vector.drain()
                                vector.tensor_tensor(rcol[:, p:p + 1],
                                                     rcol[:, p:p + 1],
                                                     tcol[:, p:p + 1], OP.mult)
                                vector.drain()
                                vector.tensor_tensor(nmcol[:, p:p + 1],
                                                     mucol[:, p:p + 1],
                                                     rcol[:, p:p + 1], OP.mult)
                                vector.drain()
                                vector.tensor_scalar(nmcol[:, p:p + 1],
                                                     nmcol[:, p:p + 1], -1.0,
                                                     None, OP.mult)
                                vector.drain()
                                W(vector, f"c_st{i % 2}", 48 * (i // 2))
                                vector.tensor_scalar(x1b[p][:], ybuf[p][:],
                                                     rcol[:, p:p + 1],
                                                     nmcol[:, p:p + 1],
                                                     OP.mult, OP.add)
                                vector.tensor_copy(x1hb[p][:], x1b[p][:])
                                vector.tensor_sub(x1lb[p][:], x1b[p][:],
                                                  x1hb[p][:])\
                                    .then_inc(sem["c_x1"], 1)
                        ph_end(vector)

                    def c1_scalar(scalar):
                        ph_start(scalar)
                        for jj in range(NT + 1):
                            if jj < NT:
                                j = jj
                                W(scalar, "c_y", j + 1)
                                scalar.activation(junk[:], ybuf[j % 2][:],
                                                  AF.Square,
                                                  accum_out=s2a[:, j:j + 1])
                                scalar.drain().then_inc(sem["c_s2"], 1)
                            if jj >= 1:
                                i = jj - 1
                                W(scalar, "c_v", i + 1)
                                scalar.activation(s0col[:, i % 2:i % 2 + 1],
                                                  vcol[:, i % 2:i % 2 + 1],
                                                  AF.Sqrt)
                                scalar.drain().then_inc(sem["c_sq"], 1)
                        ph_end(scalar)

                    def c1_gpsimd(gpsimd):
                        ph_start(gpsimd)
                        for j in range(NT):
                            p = j % 2
                            W(gpsimd, "c_x1", j + 1)
                            r = slice(j * 128, (j + 1) * 128)
                            gpsimd.dma_start(out=x1_s[r, :], in_=x1b[p][:])\
                                .then_inc(sem[f"c_st{j % 2}"], 16)
                            gpsimd.dma_start(out=x1h_s[r, :], in_=x1hb[p][:])\
                                .then_inc(sem[f"c_st{j % 2}"], 16)
                            gpsimd.dma_start(out=x1l_s[r, :], in_=x1lb[p][:])\
                                .then_inc(sem[f"c_st{j % 2}"], 16)
                        W(gpsimd, "c_st0", 48 * (NT // 2))
                        W(gpsimd, "c_st1", 48 * (NT // 2))
                        ph_end(gpsimd)

                    blk.sync(c1_sync)
                    blk.tensor(c1_tensor)
                    blk.vector(c1_vector)
                    blk.scalar(c1_scalar)
                    blk.gpsimd(c1_gpsimd)
                bar_k[0] += 1

        with ExitStack() as st3:
            wfc_sb = st3.enter_context(
                nc.sbuf_tensor("wfc_sb", [128, DC * H], BF16))
            grpa = [st3.enter_context(
                nc.sbuf_tensor(f"grpa{i}", [128, DC * 2 * 512], BF16))
                for i in range(2)]
            gbuf = [st3.enter_context(
                nc.sbuf_tensor(f"gbuf{i}", [128, H], F32)) for i in range(2)]
            gpf = st3.enter_context(nc.sbuf_tensor("gpf", [128, H], F32))
            gph = st3.enter_context(nc.sbuf_tensor("gph", [128, H], BF16))
            gpl = st3.enter_context(nc.sbuf_tensor("gpl", [128, H], BF16))
            lnwb = st3.enter_context(nc.sbuf_tensor("lnwb", [128, H], F32))
            junk = st3.enter_context(nc.sbuf_tensor("junk2", [128, H], BF16))
            s1p = [st3.enter_context(
                nc.sbuf_tensor(f"s1p{i}", [128, 8], F32)) for i in range(2)]

            with (
                nc.psum_tensor("d_h", [128, H], F32) as d_h,
                nc.Block("pc2a") as blk,
            ):
                def _xload_a(eng, g, hf):
                    rows = slice(g * 512, (g + 1) * 512)
                    src = (x1h_s, x1l_s)[hf]
                    sm = (f"d_ldT{g % 2}", f"d_ldT2{g % 2}")[hf]
                    for c in range(DC):
                        col = (c * 2 + hf) * 512
                        eng.dma_start(
                            out=grpa[g % 2][:, col:col + 512],
                            in_=src[rows, c * 128:(c + 1) * 128],
                            transpose=True,
                        ).then_inc(sem[sm], 16)

                def d_sync(sync):
                    ph_start(sync)
                    for kc in range(0, DC, 2):
                        sync.dma_start(out=wfc_sb[:, kc * H:(kc + 1) * H],
                                       in_=wfcq_s[kc])\
                            .then_inc(sem["d_wfc"], 16)
                    sync.dma_start(out=lnwb[0:1, :], in_=lnw_row_d[:, :])\
                        .then_inc(sem["d_lnw"], 16)
                    for g in range(NG_A):
                        W(sync, "d_hd", 32 * (g - 1))
                        _xload_a(sync, g, 0)
                        _xload_a(sync, g, 1)
                    ph_end(sync)

                def d_gpsimd(gpsimd):
                    ph_start(gpsimd)
                    for kc in range(1, DC, 2):
                        gpsimd.dma_start(out=wfc_sb[:, kc * H:(kc + 1) * H],
                                         in_=wfcq_s[kc])\
                            .then_inc(sem["d_wfcG"], 16)
                    W(gpsimd, "d_lnw", 16)
                    for k in range(7):
                        w = 1 << k
                        if k > 0:
                            W(gpsimd, "d_gst", 16 * k)
                        gpsimd.dma_start(out=lnwb[w:2 * w, :], in_=lnwb[0:w, :])\
                            .then_inc(sem["d_gst"], 16)
                    nst = 7
                    for j in range(NT):
                        W(gpsimd, "d_gp", j + 1)
                        r = slice(j * 128, (j + 1) * 128)
                        gpsimd.dma_start(out=gh_s[r, :], in_=gph[:])\
                            .then_inc(sem["d_gst"], 16)
                        gpsimd.dma_start(out=gl_s[r, :], in_=gpl[:])\
                            .then_inc(sem["d_gst"], 16)
                        nst += 2
                    W(gpsimd, "d_gst", 16 * nst)
                    ph_end(gpsimd)

                def d_tensor(tensor):
                    ph_start(tensor)
                    for j in range(NT):
                        g, m = j // 4, j % 4
                        W(tensor, f"d_ldT{g % 2}", 16 * 8 * (g // 2 + 1))
                        W(tensor, f"d_ldT2{g % 2}", 16 * 8 * (g // 2 + 1))
                        if j == 0:
                            W(tensor, "d_wfc", 16 * 4)
                            W(tensor, "d_wfcG", 16 * 4)
                        for kc in range(DC):
                            for hf in range(2):
                                lhsT = grpa[g % 2][
                                    :, (kc * 2 + hf) * 512 + m * 128:
                                       (kc * 2 + hf) * 512 + m * 128 + 128]
                                for nch in range(8):
                                    if kc == 0 and hf == 0 and j >= 1:
                                        W(tensor, "d_gelu", 8 * (j - 1) + nch + 1)
                                    inst = tensor.matmul(
                                        d_h[:, nch * 512:nch * 512 + 512],
                                        lhsT,
                                        wfc_sb[:, kc * H + nch * 512:
                                               kc * H + nch * 512 + 512],
                                        start=(kc == 0 and hf == 0),
                                        stop=(kc == DC - 1 and hf == 1))
                                    if kc == DC - 1 and hf == 1:
                                        inst.then_inc(sem["d_hd"], 1)
                    ph_end(tensor)

                def d_scalar(scalar):
                    ph_start(scalar)
                    for j in range(NT):
                        p = j % 2
                        for nch in range(8):
                            W(scalar, "d_hd", 8 * j + nch + 1)
                            if nch == 0:
                                W(scalar, "d_gp", j - 1)
                            scalar.activation(
                                gbuf[p][:, nch * 512:nch * 512 + 512],
                                d_h[:, nch * 512:nch * 512 + 512],
                                AF.Gelu,
                                accum_out=s1p[p][:, nch:nch + 1])\
                                .then_inc(sem["d_gelu"], 1)
                        scalar.drain().then_inc(sem["d_acc"], 1)
                    ph_end(scalar)

                def d_vector(vector):
                    ph_start(vector)
                    for j in range(NT):
                        p = j % 2
                        W(vector, "d_acc", j + 1)
                        vector.reduce_sum(s1_t[:, j:j + 1], s1p[p][:], axis=AX.X)
                        vector.scalar_tensor_tensor(
                            junk[:, 0:H], gbuf[p][:], 1.0, gbuf[p][:],
                            OP.bypass, OP.mult,
                            accum_out=s2_t[:, j:j + 1])
                        W(vector, "d_gst", 16 * (7 + 2 * j))
                        vector.tensor_tensor(gpf[:], gbuf[p][:], lnwb[:],
                                             OP.mult)
                        vector.tensor_copy(gph[:], gpf[:])
                        vector.tensor_sub(gpl[:], gpf[:], gph[:])\
                            .then_inc(sem["d_gp"], 1)
                    ph_end(vector)

                blk.sync(d_sync)
                blk.gpsimd(d_gpsimd)
                blk.tensor(d_tensor)
                blk.scalar(d_scalar)
                blk.vector(d_vector)
            bar_k[0] += 1

        with ExitStack() as st4:
            wpj_sb = st4.enter_context(
                nc.sbuf_tensor("wpj_sb", [128, HC * D], BF16))
            grpb = [st4.enter_context(
                nc.sbuf_tensor(f"grpb{i}", [128, HC * 2 * 256], BF16))
                for i in range(2)]
            x1db = [st4.enter_context(
                nc.sbuf_tensor(f"x1db{i}", [128, 2 * D], F32)) for i in range(2)]
            o1b = [st4.enter_context(
                nc.sbuf_tensor(f"o1b{i}", [128, D], F32)) for i in range(2)]
            o2b = st4.enter_context(nc.sbuf_tensor("o2b", [128, D], F32))
            x1cb = st4.enter_context(nc.sbuf_tensor("x1cb", [128, D], F32))
            y2b = [st4.enter_context(
                nc.sbuf_tensor(f"y2b{i}", [128, D], F32)) for i in range(2)]
            outb = [st4.enter_context(
                nc.sbuf_tensor(f"outb{i}", [128, D], F32)) for i in range(2)]
            c1b = st4.enter_context(nc.sbuf_tensor("c1b", [128, D], F32))
            c2b = st4.enter_context(nc.sbuf_tensor("c2b", [128, D], F32))
            r2a = st4.enter_context(nc.sbuf_tensor("r2a", [128, NT], F32))
            nm2a = st4.enter_context(nc.sbuf_tensor("nm2a", [128, NT], F32))
            tmpa = st4.enter_context(nc.sbuf_tensor("tmpa", [128, NT], F32))
            vea = st4.enter_context(nc.sbuf_tensor("vea", [128, NT], F32))
            s13 = st4.enter_context(nc.sbuf_tensor("s13", [128, 2], F32))
            s23 = st4.enter_context(nc.sbuf_tensor("s23", [128, 2], F32))
            mu3 = st4.enter_context(nc.sbuf_tensor("mu3", [128, 2], F32))
            v3 = st4.enter_context(nc.sbuf_tensor("v3", [128, 2], F32))
            s03 = st4.enter_context(nc.sbuf_tensor("s03", [128, 2], F32))
            t3 = st4.enter_context(nc.sbuf_tensor("t3", [128, 2], F32))
            r3 = st4.enter_context(nc.sbuf_tensor("r3", [128, 2], F32))
            nm3 = st4.enter_context(nc.sbuf_tensor("nm3", [128, 2], F32))
            junk = st4.enter_context(nc.sbuf_tensor("junk3", [128, D], BF16))

            with (
                nc.psum_tensor("e_o0", [128, D], F32) as e_o0,
                nc.psum_tensor("e_o1p", [128, D], F32) as e_o1p,
                nc.Block("pc2b") as blk,
            ):
                ops_ = [e_o0, e_o1p]

                def _gload_b(eng, g, hf):
                    rows = slice(g * 256, (g + 1) * 256)
                    src = (gh_s, gl_s)[hf]
                    sm = (f"e_ldT{g % 2}", f"e_ldT2{g % 2}")[hf]
                    for jc in range(HC):
                        col = (jc * 2 + hf) * 256
                        eng.dma_start(
                            out=grpb[g % 2][:, col:col + 256],
                            in_=src[rows, jc * 128:(jc + 1) * 128],
                            transpose=True,
                        ).then_inc(sem[sm], 16)

                def e_sync(sync):
                    ph_start(sync)
                    for jc in range(0, HC, 2):
                        sync.dma_start(out=wpj_sb[:, jc * D:(jc + 1) * D],
                                       in_=wprojq_s[jc])\
                            .then_inc(sem["e_wpj"], 16)
                    for g in range(NG_B):
                        W(sync, "e_mm", 2 * (g - 1))
                        _gload_b(sync, g, 0)
                        _gload_b(sync, g, 1)
                        W(sync, "e_y2", 2 * (g - 1))
                        sync.dma_start(
                            out=x1db[g % 2][:].rearrange("p (a d) -> p a d", a=2),
                            in_=x1_s[g * 256:(g + 1) * 256, :].rearrange(
                                "(a p) d -> p a d", p=128),
                        ).then_inc(sem[f"e_ldx{g % 2}"], 16)
                    ph_end(sync)

                def e_vector(vector):
                    ph_start(vector)
                    vector.tensor_scalar(tmpa[:], s1_t[:], 1.0 / H, None,
                                         OP.mult)
                    vector.drain()
                    vector.tensor_tensor(vea[:], tmpa[:], tmpa[:], OP.mult)
                    vector.drain()
                    vector.scalar_tensor_tensor(vea[:], s2_t[:], 1.0 / H,
                                                vea[:], OP.mult, OP.subtract)
                    vector.drain()
                    vector.tensor_scalar(vea[:], vea[:], LN_EPS, None, OP.add)
                    vector.drain().then_inc(sem["e_r2"], 1)
                    W(vector, "e_r2", 2)
                    vector.reciprocal(r2a[:], nm2a[:])
                    vector.drain()
                    vector.tensor_tensor(nm2a[:], r2a[:], r2a[:], OP.mult)
                    vector.drain()
                    vector.tensor_tensor(nm2a[:], nm2a[:], vea[:], OP.mult)
                    vector.drain()
                    vector.tensor_scalar(nm2a[:], nm2a[:], -0.5, 1.5,
                                         OP.mult, OP.add)
                    vector.drain()
                    vector.tensor_tensor(r2a[:], r2a[:], nm2a[:], OP.mult)
                    vector.drain()
                    vector.tensor_tensor(nm2a[:], tmpa[:], r2a[:], OP.mult)
                    vector.drain()
                    vector.tensor_scalar(nm2a[:], nm2a[:], -1.0, None, OP.mult)
                    vector.drain().then_inc(sem["e_r2"], 1)
                    W(vector, "e_c12b", 16 * 16)
                    for jj in range(NT + 2):
                        if jj >= 2:
                            i2 = jj - 2
                            p = i2 % 2
                            W(vector, "e_sq3", i2 + 1)
                            vector.reciprocal(r3[:, p:p + 1], s03[:, p:p + 1])
                            vector.drain()
                            vector.tensor_tensor(t3[:, p:p + 1], r3[:, p:p + 1],
                                                 r3[:, p:p + 1], OP.mult)
                            vector.drain()
                            vector.tensor_tensor(t3[:, p:p + 1], t3[:, p:p + 1],
                                                 v3[:, p:p + 1], OP.mult)
                            vector.drain()
                            vector.tensor_scalar(t3[:, p:p + 1], t3[:, p:p + 1],
                                                 -0.5, 1.5, OP.mult, OP.add)
                            vector.drain()
                            vector.tensor_tensor(r3[:, p:p + 1], r3[:, p:p + 1],
                                                 t3[:, p:p + 1], OP.mult)
                            vector.drain()
                            vector.tensor_tensor(nm3[:, p:p + 1],
                                                 mu3[:, p:p + 1],
                                                 r3[:, p:p + 1], OP.mult)
                            vector.drain()
                            vector.tensor_scalar(nm3[:, p:p + 1],
                                                 nm3[:, p:p + 1], -1.0,
                                                 None, OP.mult)
                            vector.drain()
                            W(vector, f"e_st{i2 % 2}", 16 * (i2 // 2))
                            vector.tensor_scalar(outb[p][:], y2b[p][:],
                                                 r3[:, p:p + 1],
                                                 nm3[:, p:p + 1],
                                                 OP.mult, OP.add)\
                                .then_inc(sem["e_out"], 1)
                        if 1 <= jj <= NT:
                            i1 = jj - 1
                            p = i1 % 2
                            W(vector, "e_s23", i1 + 1)
                            W(vector, "e_sq3", i1 - 1)
                            vector.tensor_scalar(mu3[:, p:p + 1],
                                                 s13[:, p:p + 1], 1.0 / D,
                                                 None, OP.mult)
                            vector.drain()
                            vector.tensor_tensor(t3[:, p:p + 1],
                                                 mu3[:, p:p + 1],
                                                 mu3[:, p:p + 1], OP.mult)
                            vector.drain()
                            vector.scalar_tensor_tensor(
                                v3[:, p:p + 1], s23[:, p:p + 1], 1.0 / D,
                                t3[:, p:p + 1], OP.mult, OP.subtract)
                            vector.drain()
                            vector.tensor_scalar(v3[:, p:p + 1], v3[:, p:p + 1],
                                                 LN_EPS, None, OP.add)
                            vector.drain().then_inc(sem["e_v3"], 1)
                        if jj < NT:
                            j = jj
                            p = j % 2
                            g = j // 2
                            W(vector, "e_o1", j + 1)
                            W(vector, f"e_ldx{g % 2}", 16 * (g // 2 + 1))
                            vector.scalar_tensor_tensor(
                                o2b[:], c1b[:], nm2a[:, j:j + 1], o1b[p][:],
                                OP.mult, OP.add)
                            vector.tensor_tensor(
                                x1cb[:], x1db[g % 2][:, (j % 2) * D:
                                                     (j % 2) * D + D],
                                c2b[:], OP.add)
                            W(vector, "e_s23", j - 1)
                            vector.scalar_tensor_tensor(
                                y2b[p][:], o2b[:], 0.0, x1cb[:],
                                OP.bypass, OP.add,
                                accum_out=s13[:, p:p + 1])\
                                .then_inc(sem["e_y2"], 1)
                    ph_end(vector)

                def e_scalar(scalar):
                    ph_start(scalar)
                    W(scalar, "e_r2", 1)
                    scalar.activation(nm2a[:], vea[:], AF.Sqrt)
                    scalar.drain().then_inc(sem["e_r2"], 1)
                    for jj in range(NT + 2):
                        if jj < NT:
                            j = jj
                            p = j % 2
                            W(scalar, "e_mm", j + 1)
                            W(scalar, "e_r2", 3)
                            W(scalar, "e_y2", j - 1)
                            scalar.activation(o1b[p][:], ops_[p][:], AF.Copy,
                                              scale=r2a[:, j:j + 1])\
                                .then_inc(sem["e_o1"], 1)
                        if 1 <= jj <= NT:
                            i = jj - 1
                            W(scalar, "e_y2", i + 1)
                            W(scalar, "e_v3", i - 1)
                            scalar.activation(junk[:], y2b[i % 2][:],
                                              AF.Square,
                                              accum_out=s23[:, i % 2:i % 2 + 1])
                            scalar.drain().then_inc(sem["e_s23"], 1)
                        if jj >= 2:
                            i2 = jj - 2
                            W(scalar, "e_v3", i2 + 1)
                            W(scalar, "e_out", i2 - 1)
                            scalar.activation(s03[:, i2 % 2:i2 % 2 + 1],
                                              v3[:, i2 % 2:i2 % 2 + 1],
                                              AF.Sqrt)
                            scalar.drain().then_inc(sem["e_sq3"], 1)
                    ph_end(scalar)

                def e_tensor(tensor):
                    ph_start(tensor)
                    for j in range(NT):
                        g, m = j // 2, j % 2
                        W(tensor, f"e_ldT{g % 2}", 16 * 32 * (g // 2 + 1))
                        W(tensor, f"e_ldT2{g % 2}", 16 * 32 * (g // 2 + 1))
                        W(tensor, "e_o1", j - 1)
                        if j == 0:
                            W(tensor, "e_wpj", 16 * 16)
                            W(tensor, "e_wpjG", 16 * 16)
                        inst = None
                        for jc in range(HC):
                            for hf in range(2):
                                lhsT = grpb[g % 2][
                                    :, (jc * 2 + hf) * 256 + m * 128:
                                       (jc * 2 + hf) * 256 + m * 128 + 128]
                                for nch in range(2):
                                    inst = tensor.matmul(
                                        ops_[j % 2][:, nch * 512:nch * 512 + 512],
                                        lhsT,
                                        wpj_sb[:, jc * D + nch * 512:
                                               jc * D + nch * 512 + 512],
                                        start=(jc == 0 and hf == 0),
                                        stop=(jc == HC - 1 and hf == 1))
                        inst.then_inc(sem["e_mm"], 1)
                    ph_end(tensor)

                def e_gpsimd(gpsimd):
                    ph_start(gpsimd)
                    for jc in range(1, HC, 2):
                        gpsimd.dma_start(out=wpj_sb[:, jc * D:(jc + 1) * D],
                                         in_=wprojq_s[jc])\
                            .then_inc(sem["e_wpjG"], 16)
                    gpsimd.dma_start(out=c1b[0:1, :], in_=c12_sb[0:1, :])\
                        .then_inc(sem["e_c12b"], 16)
                    gpsimd.dma_start(out=c2b[0:1, :], in_=c12_sb[1:2, :])\
                        .then_inc(sem["e_c12b"], 16)
                    for k in range(6):
                        w = 1 << k
                        W(gpsimd, "e_c12b", 16 * 2 * (k + 1))
                        gpsimd.dma_start(out=c1b[w:2 * w, :], in_=c1b[0:w, :])\
                            .then_inc(sem["e_c12b"], 16)
                        gpsimd.dma_start(out=c2b[w:2 * w, :], in_=c2b[0:w, :])\
                            .then_inc(sem["e_c12b"], 16)
                    W(gpsimd, "e_c12b", 16 * 14)
                    gpsimd.dma_start(out=c1b[64:128, :], in_=c1b[0:64, :])\
                        .then_inc(sem["e_c12b"], 16)
                    gpsimd.dma_start(out=c2b[64:128, :], in_=c2b[0:64, :])\
                        .then_inc(sem["e_c12b"], 16)
                    for j in range(NT):
                        W(gpsimd, "e_out", j + 1)
                        gpsimd.dma_start(out=out_d[j * 128:(j + 1) * 128, :],
                                         in_=outb[j % 2][:])\
                            .then_inc(sem[f"e_st{j % 2}"], 16)
                    ndbg = 0
                    if dbg:
                        for dsts in [(dbg_d[0:1, :], gq_sb[:]),
                                     (dbg_d[1:2, :], c12_sb[0:1, :]),
                                     (dbg_d[2:3, :], c12_sb[1:2, :]),
                                     (dbg2_d[:, 0:NT], s1_t[:]),
                                     (dbg2_d[:, NT:2 * NT], s2_t[:]),
                                     (dbg2_d[:, 2 * NT:3 * NT], r2a[:]),
                                     (dbg2_d[:, 3 * NT:4 * NT], nm2a[:])]:
                            gpsimd.dma_start(out=dsts[0], in_=dsts[1])\
                                .then_inc(sem["e_std"], 16)
                            ndbg += 1
                    W(gpsimd, "e_st0", 16 * (NT // 2))
                    W(gpsimd, "e_st1", 16 * (NT // 2))
                    if ndbg:
                        W(gpsimd, "e_std", 16 * ndbg)
                    ph_end(gpsimd)

                blk.sync(e_sync)
                blk.vector(e_vector)
                blk.scalar(e_scalar)
                blk.tensor(e_tensor)
                blk.gpsimd(e_gpsimd)
            bar_k[0] += 1

    return nc


def kernel(L0_attn_x, L0_attn_Wq, L0_attn_Wk, L0_attn_Wv, L0_attn_Wr,
           L0_ffn_W_fc, L0_ffn_ln_w, L0_ffn_ln_b, L0_ffn_W_proj):
    B, S, D_ = L0_attn_x.shape
    assert D_ == D
    x = np.ascontiguousarray(np.asarray(L0_attn_x, dtype=np.float32))
    f32 = lambda a: np.ascontiguousarray(np.asarray(a, dtype=np.float32))
    wq = f32(L0_attn_Wq)
    wk = f32(L0_attn_Wk)
    wvT = np.ascontiguousarray(f32(L0_attn_Wv).T)
    wr = f32(L0_attn_Wr)
    wfcT = np.ascontiguousarray(f32(L0_ffn_W_fc).T)
    wprojT = np.ascontiguousarray(f32(L0_ffn_W_proj).T)
    lnw = f32(L0_ffn_ln_w).reshape(-1)
    lnb = f32(L0_ffn_ln_b).reshape(-1)
    lnw_row = np.ascontiguousarray(lnw.reshape(1, H))
    lnwT = np.ascontiguousarray(lnw.reshape(HC, 128).T)
    lnbT = np.ascontiguousarray(lnb.reshape(HC, 128).T)

    nc = build(S)
    in_maps = []
    for i in range(B):
        in_maps.append({
            "x": np.ascontiguousarray(x[i]),
            "xT": np.ascontiguousarray(x[i].T),
            "wq": wq, "wk": wk, "wvT": wvT, "wr": wr,
            "wfcT": wfcT, "wprojT": wprojT,
            "lnw_row": lnw_row, "lnwT": lnwT, "lnbT": lnbT,
        })
    res = run_bass_kernel_spmd(nc, in_maps, core_ids=list(range(B)))
    out = np.stack([res.results[i]["out"] for i in range(B)], axis=0)
    return out.astype(np.float32)


# version 06
# speedup vs baseline: 1.0520x; 1.0520x over previous
import numpy as np
from contextlib import ExitStack

import concourse.bass as bass
import concourse.mybir as mybir
from concourse.bass_utils import run_bass_kernel_spmd
from concourse.alu_op_type import AluOpType as OP

F32 = mybir.dt.float32
F32R = mybir.dt.float32r
BF16 = mybir.dt.bfloat16
AF = mybir.ActivationFunctionType
AX = mybir.AxisListType

D = 1024
H = 4096
DC = D // 128
HC = H // 128
SCALE = 1.0 / 32.0
LN_EPS = 1e-5
Q_EPS = 1e-6


def build(S=8192, dbg=False):
    NT = S // 128
    NG_A = NT // 4
    NG_B = NT // 2
    nc = bass.Bass("TRN2", target_bir_lowering=False, debug=False,
                   detect_race_conditions=False)

    x_d = nc.dram_tensor("x", [S, D], F32, kind="ExternalInput").ap()
    xT_d = nc.dram_tensor("xT", [D, S], F32, kind="ExternalInput").ap()
    wq_d = nc.dram_tensor("wq", [D, D], F32, kind="ExternalInput").ap()
    wk_d = nc.dram_tensor("wk", [D, D], F32, kind="ExternalInput").ap()
    wvT_d = nc.dram_tensor("wvT", [D, D], F32, kind="ExternalInput").ap()
    wr_d = nc.dram_tensor("wr", [D, D], F32, kind="ExternalInput").ap()
    wfcT_d = nc.dram_tensor("wfcT", [D, H], F32, kind="ExternalInput").ap()
    wprojT_d = nc.dram_tensor("wprojT", [H, D], F32, kind="ExternalInput").ap()
    lnw_row_d = nc.dram_tensor("lnw_row", [1, H], F32, kind="ExternalInput").ap()
    lnwT_d = nc.dram_tensor("lnwT", [128, HC], F32, kind="ExternalInput").ap()
    lnbT_d = nc.dram_tensor("lnbT", [128, HC], F32, kind="ExternalInput").ap()
    out_d = nc.dram_tensor("out", [S, D], F32, kind="ExternalOutput").ap()

    KI = "ExternalOutput" if dbg else "Internal"
    k_s = nc.dram_tensor("k_s", [S, D], F32, kind=KI).ap()
    xtp_s = nc.dram_tensor("xtp_s", [NT, 128, 2048], BF16, kind=KI).ap()
    x1_s = nc.dram_tensor("x1_s", [S, D], F32, kind=KI).ap()
    x1h_s = nc.dram_tensor("x1h_s", [S, D], BF16, kind=KI).ap()
    x1l_s = nc.dram_tensor("x1l_s", [S, D], BF16, kind=KI).ap()
    gT_s = nc.dram_tensor("gT_s", [NT, 128, HC * 2 * 128], BF16, kind=KI).ap()
    wfcq_s = nc.dram_tensor("wfcq_s", [DC, 128, H], BF16, kind=KI).ap()
    wprojq_s = nc.dram_tensor("wprojq_s", [HC, 128, D], BF16, kind=KI).ap()
    if dbg:
        dbg_d = nc.dram_tensor("dbg", [4, D], F32, kind="ExternalOutput").ap()
        dbg2_d = nc.dram_tensor("dbg2", [128, 4 * NT], F32,
                                kind="ExternalOutput").ap()

    xT_r = xT_d.rearrange("(c p) s -> p c s", p=128)

    es = ExitStack()
    with es:
        sem = {}
        for name in [
            "bar",
            "p_ld0", "p_ld1", "p_cons", "p_gr", "p_bc", "p_qt",
            "p_qc0", "p_qc1", "p_mm", "p_lnld", "p_lnp", "p_spl",
            "a_ldx0", "a_ldx1", "a_split", "a_qk", "a_exp", "a_w2", "a_gq",
            "a_kev", "a_pst0", "a_pst1", "a_kst0", "a_kst1", "a_gqev",
            "b_ldk0", "b_ldk1", "b_p", "b_e", "b_w2", "b_gk", "b_gkt",
            "b_wld0", "b_wld1", "b_wsplit", "b_wc", "b_wsum", "b_gqb",
            "c_ldp0", "c_ldp1", "c_ldx0", "c_ldx1", "c_a", "c_y", "c_s2",
            "c_v", "c_sq", "c_x1", "c_st0", "c_st1",
            "d_wfc", "d_wfcG", "d_lnw", "d_ldT0", "d_ldT1", "d_ldT20",
            "d_ldT21", "d_hd", "d_gelu", "d_gp", "d_gst", "d_acc", "d_trb", "d_tev", "d_st0", "d_st1",
            "e_wpj", "e_wpjG", "e_ldT0", "e_ldT1", "e_ldT20", "e_ldT21",
            "e_ldx0", "e_ldx1", "e_r2", "e_mm", "e_o1",
            "e_y2", "e_s23", "e_v3", "e_sq3", "e_out", "e_st0", "e_st1",
            "e_std", "e_c12b",
        ]:
            sem[name] = es.enter_context(nc.semaphore(name))

        BAR = sem["bar"]
        bar_k = [0]

        def ph_start(eng):
            if bar_k[0] > 0:
                eng.wait_ge(BAR, 5 * bar_k[0])

        def ph_end(eng):
            eng.drain().then_inc(BAR, 1)

        def W(eng, s, v):
            if v > 0:
                eng.wait_ge(sem[s], v)

        ones_r = es.enter_context(nc.sbuf_tensor("ones_r", [128, 2], F32R))
        ones_f = es.enter_context(nc.sbuf_tensor("ones_f", [128, 2], F32))
        gq_sb = es.enter_context(nc.sbuf_tensor("gq_sb", [1, D], F32))
        c12_sb = es.enter_context(nc.sbuf_tensor("c12_sb", [2, D], F32))
        s1_t = es.enter_context(nc.sbuf_tensor("s1_t", [128, NT], F32))
        id_sb = es.enter_context(nc.sbuf_tensor("id_sb", [128, 128], BF16))
        s2_t = es.enter_context(nc.sbuf_tensor("s2_t", [128, NT], F32))

        with ExitStack() as st0:
            wq_hi = st0.enter_context(nc.sbuf_tensor("wq_hi", [128, D * DC], BF16))
            wq_lo = st0.enter_context(nc.sbuf_tensor("wq_lo", [128, D * DC], BF16))
            wk_hi = st0.enter_context(nc.sbuf_tensor("wk_hi", [128, D * DC], BF16))
            wk_lo = st0.enter_context(nc.sbuf_tensor("wk_lo", [128, D * DC], BF16))
            wtile = [st0.enter_context(nc.sbuf_tensor(f"wtile{i}", [128, H], F32))
                     for i in range(2)]
            qtile = [st0.enter_context(nc.sbuf_tensor(f"qtile{i}", [128, H], BF16))
                     for i in range(2)]
            gcol = st0.enter_context(nc.sbuf_tensor("gcol", [128, 40], F32))
            gred = st0.enter_context(nc.sbuf_tensor("gred", [128, 1], F32))
            gtot = st0.enter_context(nc.sbuf_tensor("gtot", [1, 128], F32))
            thrT = st0.enter_context(nc.sbuf_tensor("thrT", [1, 4], F32))
            thr_c = st0.enter_context(nc.sbuf_tensor("thr_c", [128, 2], F32))
            lnpair = st0.enter_context(nc.sbuf_tensor("lnpair", [128, 4 * HC], BF16))
            lnT_f = st0.enter_context(nc.sbuf_tensor("lnT_f", [128, 2 * HC], F32))

            def tile_src(t):
                if t < 8:
                    return wfcT_d[(t % 8) * 128:(t % 8) * 128 + 128, :], H
                if t < 16:
                    return wfcT_d[(t % 8) * 128:(t % 8) * 128 + 128, :], H
                if t < 48:
                    i = t - 16
                    return wprojT_d[i * 128:(i + 1) * 128, :], D
                if t < 80:
                    i = t - 48
                    return wprojT_d[i * 128:(i + 1) * 128, :], D
                if t < 88:
                    i = t - 80
                    return wq_d[i * 128:(i + 1) * 128, :], D
                i = t - 88
                return wk_d[i * 128:(i + 1) * 128, :], D

            with (
                nc.psum_tensor("p0_c12", [2, D], F32) as p0_c12,
                nc.Block("p0") as blk,
            ):
                def p0_sync(sync):
                    ph_start(sync)
                    sync.dma_start(out=lnT_f[:, 0:HC], in_=lnwT_d[:, :])\
                        .then_inc(sem["p_lnld"], 16)
                    sync.dma_start(out=lnT_f[:, HC:2 * HC], in_=lnbT_d[:, :])\
                        .then_inc(sem["p_lnld"], 16)
                    for t in range(96):
                        if t < 82:
                            W(sync, "p_cons", t - 1)
                        else:
                            W(sync, "p_spl", t - 81)
                        src, w = tile_src(t)
                        sync.dma_start(out=wtile[t % 2][:, 0:w], in_=src)\
                            .then_inc(sem[f"p_ld{t % 2}"], 16)
                    ph_end(sync)

                def quant_ops(vector, t, w, qk):
                    W(vector, f"p_qc{qk % 2}", 16 * (qk // 2))
                    if qk - 2 >= 8:
                        W(vector, "p_mm", qk - 9)
                    vector.tensor_scalar(qtile[t % 2][:, 0:w],
                                         wtile[t % 2][:, 0:w],
                                         thr_c[:, 0:1], None, OP.is_gt)
                    vector.scalar_tensor_tensor(
                        qtile[t % 2][:, 0:w], wtile[t % 2][:, 0:w],
                        thr_c[:, 1:2], qtile[t % 2][:, 0:w],
                        OP.is_lt, OP.subtract)\
                        .then_inc(sem["p_cons"], 1)
                    vector.tensor_scalar(qtile[t % 2][:, 0:w],
                                         qtile[t % 2][:, 0:w],
                                         -1.0, None, OP.mult)\
                        .then_inc(sem["p_qt"], 1)

                def p0_vector(vector):
                    ph_start(vector)
                    vector.memset(ones_f[:], 1.0)
                    vector.tensor_copy(ones_r[:], ones_f[:])
                    for i in range(8):
                        W(vector, f"p_ld{i % 2}", 16 * (i // 2 + 1))
                        vector.reduce_sum(gcol[:, i:i + 1], wtile[i % 2][:],
                                          axis=AX.X, apply_absolute_value=True)\
                            .then_inc(sem["p_cons"], 1)
                    vector.drain()
                    vector.reduce_sum(gred[:], gcol[:, 0:8], axis=AX.X)\
                        .then_inc(sem["p_gr"], 1)
                    W(vector, "p_bc", 16)
                    vector.reduce_sum(thrT[:, 0:1], gtot[:], axis=AX.X)
                    vector.drain()
                    vector.tensor_scalar(thrT[:, 0:1], thrT[:, 0:1],
                                         1.0 / (D * H), None, OP.mult)
                    vector.drain()
                    vector.tensor_scalar(thrT[:, 0:1], thrT[:, 0:1], Q_EPS, 0.5,
                                         OP.add, OP.mult)
                    vector.drain().then_inc(sem["p_gr"], 1)
                    W(vector, "p_bc", 16 * 9)
                    vector.tensor_scalar(thr_c[:, 1:2], thr_c[:, 0:1], -1.0,
                                         None, OP.mult)
                    vector.drain()
                    for i in range(8):
                        t = 8 + i
                        W(vector, f"p_ld{t % 2}", 16 * (t // 2 + 1))
                        quant_ops(vector, t, H, i)
                    for i in range(32):
                        t = 16 + i
                        W(vector, f"p_ld{t % 2}", 16 * (t // 2 + 1))
                        vector.reduce_sum(gcol[:, 8 + i:9 + i],
                                          wtile[t % 2][:, 0:D],
                                          axis=AX.X, apply_absolute_value=True)\
                            .then_inc(sem["p_cons"], 1)
                    vector.drain()
                    vector.reduce_sum(gred[:], gcol[:, 8:40], axis=AX.X)\
                        .then_inc(sem["p_gr"], 1)
                    W(vector, "p_bc", 16 * 10)
                    vector.reduce_sum(thrT[:, 2:3], gtot[:], axis=AX.X)
                    vector.drain()
                    vector.tensor_scalar(thrT[:, 2:3], thrT[:, 2:3],
                                         1.0 / (D * H), None, OP.mult)
                    vector.drain()
                    vector.tensor_scalar(thrT[:, 2:3], thrT[:, 2:3], Q_EPS, 0.5,
                                         OP.add, OP.mult)
                    vector.drain().then_inc(sem["p_gr"], 1)
                    W(vector, "p_bc", 16 * 18)
                    vector.tensor_scalar(thr_c[:, 1:2], thr_c[:, 0:1], -1.0,
                                         None, OP.mult)
                    vector.drain()
                    W(vector, "p_lnld", 32)
                    pv = lnpair[:].rearrange("p (h c s) -> p h c s", h=2, s=2)
                    vector.tensor_copy(pv[:, 0, :, 0], lnT_f[:, 0:HC])
                    vector.tensor_copy(pv[:, 0, :, 1], lnT_f[:, HC:2 * HC])
                    vector.tensor_sub(pv[:, 1, :, 0], lnT_f[:, 0:HC],
                                      pv[:, 0, :, 0])
                    vector.tensor_sub(pv[:, 1, :, 1], lnT_f[:, HC:2 * HC],
                                      pv[:, 0, :, 1])
                    vector.drain().then_inc(sem["p_lnp"], 1)
                    for i in range(32):
                        t = 48 + i
                        W(vector, f"p_ld{t % 2}", 16 * (t // 2 + 1))
                        quant_ops(vector, t, D, 8 + i)
                    for i in range(16):
                        t = 80 + i
                        W(vector, f"p_ld{t % 2}", 16 * (t // 2 + 1))
                        hi = wq_hi if i < 8 else wk_hi
                        lo = wq_lo if i < 8 else wk_lo
                        c = (i % 8) * D
                        vector.tensor_copy(hi[:, c:c + D], wtile[t % 2][:, 0:D])
                        vector.tensor_sub(lo[:, c:c + D], wtile[t % 2][:, 0:D],
                                          hi[:, c:c + D])\
                            .then_inc(sem["p_spl"], 1)
                    W(vector, "p_mm", 32)
                    vector.tensor_copy(c12_sb[:], p0_c12[:])
                    ph_end(vector)

                def p0_tensor(tensor):
                    ph_start(tensor)
                    W(tensor, "p_lnp", 1)
                    for i in range(32):
                        W(tensor, "p_qt", 8 + i + 1)
                        t = 48 + i
                        inst = None
                        for half in range(2):
                            c = (half * HC + i) * 2
                            for nch in range(2):
                                cs = slice(nch * 512, nch * 512 + 512)
                                inst = tensor.matmul(
                                    p0_c12[:, cs], lnpair[:, c:c + 2],
                                    qtile[t % 2][:, 0:D][:, cs],
                                    start=(i == 0 and half == 0),
                                    stop=(i == 31 and half == 1))
                        inst.then_inc(sem["p_mm"], 1)
                    ph_end(tensor)

                def p0_scalar(scalar):
                    ph_start(scalar)
                    ph_end(scalar)

                def p0_gpsimd(gpsimd):
                    ph_start(gpsimd)
                    gpsimd.memset(id_sb[:], 0.0)
                    gpsimd.affine_select(
                        out=id_sb[:], in_=id_sb[:],
                        compare_op=OP.not_equal, fill=1.0, base=0,
                        pattern=[[-1, 128]], channel_multiplier=1)

                    def tree(dst, srcrow, base):
                        gpsimd.dma_start(out=dst[0:1, 0:1], in_=srcrow)\
                            .then_inc(sem["p_bc"], 16)
                        for k in range(6):
                            w = 1 << k
                            W(gpsimd, "p_bc", base + 16 * (k + 1))
                            gpsimd.dma_start(out=dst[w:2 * w, 0:1],
                                             in_=dst[0:w, 0:1])\
                                .then_inc(sem["p_bc"], 16)
                        W(gpsimd, "p_bc", base + 16 * 7)
                        gpsimd.dma_start(out=dst[64:128, 0:1],
                                         in_=dst[0:64, 0:1])\
                            .then_inc(sem["p_bc"], 16)

                    W(gpsimd, "p_gr", 1)
                    gpsimd.dma_start(out=gtot[0:1, :], in_=gred[:, 0:1])\
                        .then_inc(sem["p_bc"], 16)
                    W(gpsimd, "p_gr", 2)
                    tree(thr_c, thrT[0:1, 0:1], 16)
                    for i in range(8):
                        W(gpsimd, "p_qt", i + 1)
                        gpsimd.dma_start(out=wfcq_s[i],
                                         in_=qtile[i % 2][:])\
                            .then_inc(sem[f"p_qc{i % 2}"], 16)
                    W(gpsimd, "p_gr", 3)
                    gpsimd.dma_start(out=gtot[0:1, :], in_=gred[:, 0:1])\
                        .then_inc(sem["p_bc"], 16)
                    W(gpsimd, "p_gr", 4)
                    tree(thr_c, thrT[0:1, 2:3], 160)
                    for i in range(32):
                        W(gpsimd, "p_qt", 8 + i + 1)
                        gpsimd.dma_start(out=wprojq_s[i],
                                         in_=qtile[i % 2][:, 0:D])\
                            .then_inc(sem[f"p_qc{i % 2}"], 16)
                    W(gpsimd, "p_qc0", 16 * 20)
                    W(gpsimd, "p_qc1", 16 * 20)
                    ph_end(gpsimd)

                blk.sync(p0_sync)
                blk.vector(p0_vector)
                blk.tensor(p0_tensor)
                blk.scalar(p0_scalar)
                blk.gpsimd(p0_gpsimd)
            bar_k[0] += 1

            xtile = [st0.enter_context(nc.sbuf_tensor(f"xtile{i}", [128, D], F32))
                     for i in range(2)]
            pairb = [st0.enter_context(nc.sbuf_tensor(f"pairb{i}", [128, 2048], BF16))
                     for i in range(2)]
            ebuf = [st0.enter_context(nc.sbuf_tensor(f"ebuf{i}", [128, D], F32))
                    for i in range(2)]
            dcol = st0.enter_context(nc.sbuf_tensor("dcol", [128, 2], F32))
            rdcol = st0.enter_context(nc.sbuf_tensor("rdcol", [128, 2], F32))
            w2b = [st0.enter_context(nc.sbuf_tensor(f"w2b{i}", [128, D], F32R))
                   for i in range(2)]
            kev = [st0.enter_context(nc.sbuf_tensor(f"kev{i}", [128, D], F32))
                   for i in range(2)]

            with (
                nc.psum_tensor("pa_q0", [128, D], F32) as pa_q0,
                nc.psum_tensor("pa_q1", [128, D], F32) as pa_q1,
                nc.psum_tensor("pa_k", [128, D], F32) as pa_k,
                nc.psum_tensor("pa_gq", [1, D], F32) as pa_gq,
                nc.Block("pa") as blk,
            ):
                qps = [pa_q0, pa_q1]
                passes = [(0, 0), (1, 0), (0, 1)]

                def pa_sync(sync):
                    ph_start(sync)
                    for j in range(NT):
                        W(sync, "a_split", 2 * (j - 1))
                        sync.dma_start(
                            out=xtile[j % 2][:].rearrange("p (c t) -> p c t", c=DC),
                            in_=xT_r[:, :, j * 128:(j + 1) * 128],
                        ).then_inc(sem[f"a_ldx{j % 2}"], 16)
                    ph_end(sync)

                def pa_vector(vector):
                    ph_start(vector)
                    for jj in range(NT + 1):
                        if jj < NT:
                            j = jj
                            W(vector, f"a_ldx{j % 2}", 16 * (j // 2 + 1))
                            W(vector, "a_qk", 2 * (j - 1))
                            W(vector, f"a_pst{j % 2}", 16 * (j // 2))
                            dst = pairb[j % 2][:].rearrange(
                                "p (c h t) -> p c h t", c=DC, h=2)
                            src = xtile[j % 2][:].rearrange(
                                "p (c t) -> p c t", c=DC)
                            vector.tensor_copy(dst[:, :, 0, :], src)
                            vector.tensor_tensor(dst[:, :, 1, :], src,
                                                 dst[:, :, 0, :], OP.subtract)\
                                .then_inc(sem["a_split"], 2)
                        if jj >= 1:
                            i = jj - 1
                            p = i % 2
                            W(vector, "a_exp", i + 1)
                            vector.reciprocal(rdcol[:, p:p + 1], dcol[:, p:p + 1])
                            vector.drain()
                            W(vector, "a_gq", i - 1)
                            vector.scalar_tensor_tensor(
                                w2b[p][:], qps[p][:], rdcol[:, p:p + 1],
                                ebuf[p][:], OP.mult, OP.mult)\
                                .then_inc(sem["a_w2"], 1)
                            W(vector, "a_qk", 2 * i + 2)
                            W(vector, f"a_kst{i % 2}", 16 * (i // 2))
                            vector.tensor_copy(kev[p][:], pa_k[:])\
                                .then_inc(sem["a_kev"], 1)
                    W(vector, "a_gq", NT)
                    vector.tensor_copy(gq_sb[:], pa_gq[:])
                    vector.drain().then_inc(sem["a_gqev"], 1)
                    ph_end(vector)

                def pa_scalar(scalar):
                    ph_start(scalar)
                    for j in range(NT):
                        p = j % 2
                        W(scalar, "a_qk", 2 * j + 1)
                        W(scalar, "a_w2", j - 1)
                        scalar.activation(ebuf[p][:], qps[p][:], AF.Exp,
                                          scale=SCALE,
                                          accum_out=dcol[:, p:p + 1])
                        scalar.drain().then_inc(sem["a_exp"], 1)
                    ph_end(scalar)

                def _mm_block(tensor, psum, pb, whi, wlo, j0):
                    inst = None
                    for ip, (lh, rh) in enumerate(passes):
                        w = [whi, wlo][rh]
                        for kc in range(DC):
                            lhsT = pb[:, kc * 256 + lh * 128:
                                      kc * 256 + lh * 128 + 128]
                            for nch in range(2):
                                inst = tensor.matmul(
                                    psum[:, nch * 512:nch * 512 + 512],
                                    lhsT,
                                    w[:, kc * D + nch * 512:
                                      kc * D + nch * 512 + 512],
                                    start=(ip == 0 and kc == 0),
                                    stop=(ip == 2 and kc == DC - 1))
                    return inst

                def _pa_gq_mm(tensor, j):
                    W(tensor, "a_w2", j + 1)
                    rhs = w2b[j % 2][:]
                    inst = None
                    for nch in range(2):
                        inst = tensor.matmul(
                            pa_gq[:, nch * 512:nch * 512 + 512],
                            ones_r[:, 0:1], rhs[:, nch * 512:nch * 512 + 512],
                            start=(j == 0), stop=(j == NT - 1))
                    inst.then_inc(sem["a_gq"], 1)

                def pa_tensor(tensor):
                    ph_start(tensor)
                    for j in range(NT):
                        W(tensor, "a_split", 2 * (j + 1))
                        W(tensor, "a_w2", j - 1)
                        pb = pairb[j % 2][:]
                        _mm_block(tensor, qps[j % 2][:], pb, wq_hi, wq_lo, j)\
                            .then_inc(sem["a_qk"], 1)
                        W(tensor, "a_kev", j)
                        _mm_block(tensor, pa_k[:], pb, wk_hi, wk_lo, j)\
                            .then_inc(sem["a_qk"], 1)
                        if j >= 1:
                            _pa_gq_mm(tensor, j - 1)
                    _pa_gq_mm(tensor, NT - 1)
                    ph_end(tensor)

                def pa_gpsimd(gpsimd):
                    ph_start(gpsimd)
                    for j in range(NT):
                        W(gpsimd, "a_split", 2 * (j + 1))
                        gpsimd.dma_start(out=xtp_s[j], in_=pairb[j % 2][:])\
                            .then_inc(sem[f"a_pst{j % 2}"], 16)
                        W(gpsimd, "a_kev", j + 1)
                        gpsimd.dma_start(out=k_s[j * 128:(j + 1) * 128, :],
                                         in_=kev[j % 2][:])\
                            .then_inc(sem[f"a_kst{j % 2}"], 16)
                    W(gpsimd, "a_pst0", 16 * (NT // 2))
                    W(gpsimd, "a_pst1", 16 * (NT // 2))
                    W(gpsimd, "a_kst0", 16 * (NT // 2))
                    W(gpsimd, "a_kst1", 16 * (NT // 2))
                    ph_end(gpsimd)

                blk.sync(pa_sync)
                blk.vector(pa_vector)
                blk.scalar(pa_scalar)
                blk.tensor(pa_tensor)
                blk.gpsimd(pa_gpsimd)
            bar_k[0] += 1

        with ExitStack() as stw:
            wsum_hi = stw.enter_context(
                nc.sbuf_tensor("wsum_hi", [128, D * DC], BF16))
            wsum_lo = stw.enter_context(
                nc.sbuf_tensor("wsum_lo", [128, D * DC], BF16))

            with ExitStack() as st1:
                kbuf = [st1.enter_context(nc.sbuf_tensor(f"kbuf{i}", [128, D], F32))
                        for i in range(2)]
                pbuf = [st1.enter_context(nc.sbuf_tensor(f"pbuf{i}", [128, D], F32))
                        for i in range(2)]
                ebuf2 = [st1.enter_context(nc.sbuf_tensor(f"ebuf2{i}", [128, D], F32))
                         for i in range(2)]
                w2b2 = [st1.enter_context(nc.sbuf_tensor(f"w2b2{i}", [128, D], F32R))
                        for i in range(2)]
                dcol2 = st1.enter_context(nc.sbuf_tensor("dcol2", [128, 2], F32))
                rdcol2 = st1.enter_context(nc.sbuf_tensor("rdcol2", [128, 2], F32))
                gq_b = st1.enter_context(nc.sbuf_tensor("gq_b", [128, D], F32))
                gkt_sb = st1.enter_context(nc.sbuf_tensor("gkt_sb", [128, DC], F32))
                wvts_hi = st1.enter_context(
                    nc.sbuf_tensor("wvts_hi", [128, D * DC], BF16))
                wvts_lo = st1.enter_context(
                    nc.sbuf_tensor("wvts_lo", [128, D * DC], BF16))
                wr_hi = st1.enter_context(
                    nc.sbuf_tensor("wr_hi", [128, D * DC], BF16))
                wr_lo = st1.enter_context(
                    nc.sbuf_tensor("wr_lo", [128, D * DC], BF16))
                wraw = [st1.enter_context(
                    nc.sbuf_tensor(f"wraw{i}", [128, D], F32)) for i in range(2)]
                wsf = st1.enter_context(nc.sbuf_tensor("wsf", [128, D], F32))

                with (
                    nc.psum_tensor("pb_gkt", [128, 2 * DC], F32) as pb_gkt,
                    nc.psum_tensor("pb_wc0", [128, D], F32) as pb_wc0,
                    nc.psum_tensor("pb_wc1", [128, D], F32) as pb_wc1,
                    nc.Block("pb") as blk,
                ):
                    wcps = [pb_wc0, pb_wc1]

                    def pb_sync(sync):
                        ph_start(sync)
                        for j in range(NT):
                            W(sync, "b_p", j - 1)
                            sync.dma_start(
                                out=kbuf[j % 2][:],
                                in_=k_s[j * 128:(j + 1) * 128, :],
                            ).then_inc(sem[f"b_ldk{j % 2}"], 16)
                        for i in range(2 * DC):
                            W(sync, "b_wsplit", 2 * (i - 1))
                            src = wvT_d if i < DC else wr_d
                            r = (i % DC) * 128
                            sync.dma_start(out=wraw[i % 2][:],
                                           in_=src[r:r + 128, :])\
                                .then_inc(sem[f"b_wld{i % 2}"], 16)
                        for dc in range(DC):
                            W(sync, "b_wsplit", 4 * DC)
                            W(sync, "b_wsum", dc - 1)
                            sync.dma_start(out=wraw[dc % 2][:],
                                           in_=wq_d[dc * 128:(dc + 1) * 128, :])\
                                .then_inc(sem[f"b_wld{dc % 2}"], 16)
                        ph_end(sync)

                    def pb_vector(vector):
                        ph_start(vector)
                        W(vector, "b_gqb", 16 * 8)
                        for jj in range(NT + 1):
                            if jj < NT:
                                j = jj
                                W(vector, f"b_ldk{j % 2}", 16 * (j // 2 + 1))
                                W(vector, "b_e", j - 1)
                                vector.tensor_tensor(pbuf[j % 2][:],
                                                     kbuf[j % 2][:],
                                                     gq_b[:], OP.mult)\
                                    .then_inc(sem["b_p"], 1)
                            if jj >= 1:
                                i = jj - 1
                                p = i % 2
                                W(vector, "b_e", i + 1)
                                vector.reciprocal(rdcol2[:, p:p + 1],
                                                  dcol2[:, p:p + 1])
                                vector.drain()
                                W(vector, "b_gk", i - 1)
                                vector.scalar_tensor_tensor(
                                    w2b2[p][:], pbuf[p][:],
                                    rdcol2[:, p:p + 1], ebuf2[p][:],
                                    OP.mult, OP.mult)\
                                    .then_inc(sem["b_w2"], 1)
                        W(vector, "b_gk", NT)
                        vector.tensor_copy(
                            gkt_sb[:],
                            pb_gkt[:].rearrange("p (c two) -> p c two",
                                                two=2)[:, :, 0:1])
                        vector.drain().then_inc(sem["b_gkt"], 1)
                        for i in range(2 * DC):
                            W(vector, f"b_wld{i % 2}", 16 * (i // 2 + 1))
                            c = (i % DC) * D
                            if i < DC:
                                vector.tensor_scalar(
                                    wraw[i % 2][:], wraw[i % 2][:],
                                    gkt_sb[:, i:i + 1], None, OP.mult)
                                vector.tensor_copy(wvts_hi[:, c:c + D],
                                                   wraw[i % 2][:])
                                vector.tensor_sub(wvts_lo[:, c:c + D],
                                                  wraw[i % 2][:],
                                                  wvts_hi[:, c:c + D])\
                                    .then_inc(sem["b_wsplit"], 2)
                            else:
                                vector.tensor_copy(wr_hi[:, c:c + D],
                                                   wraw[i % 2][:])
                                vector.tensor_sub(wr_lo[:, c:c + D],
                                                  wraw[i % 2][:],
                                                  wr_hi[:, c:c + D])\
                                    .then_inc(sem["b_wsplit"], 2)
                        for dc in range(DC):
                            W(vector, "b_wc", dc + 1)
                            W(vector, f"b_wld{dc % 2}",
                              16 * ((2 * DC + dc) // 2 + 1))
                            vector.tensor_tensor(wsf[:], wcps[dc % 2][:],
                                                 wraw[dc % 2][:], OP.add)
                            c = dc * D
                            vector.tensor_copy(wsum_hi[:, c:c + D], wsf[:])
                            vector.tensor_sub(wsum_lo[:, c:c + D], wsf[:],
                                              wsum_hi[:, c:c + D])\
                                .then_inc(sem["b_wsum"], 1)
                        ph_end(vector)

                    def pb_scalar(scalar):
                        ph_start(scalar)
                        for j in range(NT):
                            p = j % 2
                            W(scalar, "b_p", j + 1)
                            W(scalar, "b_w2", j - 1)
                            scalar.activation(
                                ebuf2[p][:], pbuf[p][:], AF.Exp, scale=SCALE,
                                accum_out=dcol2[:, p:p + 1])
                            scalar.drain().then_inc(sem["b_e"], 1)
                        ph_end(scalar)

                    def pb_tensor(tensor):
                        ph_start(tensor)
                        for j in range(NT):
                            W(tensor, "b_w2", j + 1)
                            rhs = w2b2[j % 2][:]
                            inst = None
                            for c in range(DC):
                                inst = tensor.matmul(
                                    pb_gkt[:, 2 * c:2 * c + 2],
                                    rhs[:, c * 128:(c + 1) * 128],
                                    ones_r[:, 0:2],
                                    start=(j == 0 and c == 0),
                                    stop=(j == NT - 1 and c == DC - 1))
                            inst.then_inc(sem["b_gk"], 1)
                        for dc in range(DC):
                            W(tensor, "b_wsplit", 4 * DC)
                            W(tensor, "b_wsum", dc - 1)
                            inst = None
                            for ip, (lh, rh) in enumerate(
                                    [(0, 0), (1, 0), (0, 1)]):
                                lhs_t = [wvts_hi, wvts_lo][lh]
                                rhs_t = [wr_hi, wr_lo][rh]
                                for ec in range(DC):
                                    lhsT = lhs_t[:, ec * D + dc * 128:
                                                 ec * D + dc * 128 + 128]
                                    for nch in range(2):
                                        inst = tensor.matmul(
                                            wcps[dc % 2][:, nch * 512:
                                                         nch * 512 + 512],
                                            lhsT,
                                            rhs_t[:, ec * D + nch * 512:
                                                  ec * D + nch * 512 + 512],
                                            start=(ip == 0 and ec == 0),
                                            stop=(ip == 2 and ec == DC - 1))
                            inst.then_inc(sem["b_wc"], 1)
                        ph_end(tensor)

                    def pb_gpsimd(gpsimd):
                        ph_start(gpsimd)
                        W(gpsimd, "a_gqev", 1)
                        gpsimd.dma_start(out=gq_b[0:1, :], in_=gq_sb[:])\
                            .then_inc(sem["b_gqb"], 16)
                        for k in range(6):
                            w = 1 << k
                            W(gpsimd, "b_gqb", 16 * (k + 1))
                            gpsimd.dma_start(out=gq_b[w:2 * w, :],
                                             in_=gq_b[0:w, :])\
                                .then_inc(sem["b_gqb"], 16)
                        W(gpsimd, "b_gqb", 16 * 7)
                        gpsimd.dma_start(out=gq_b[64:128, :], in_=gq_b[0:64, :])\
                            .then_inc(sem["b_gqb"], 16)
                        W(gpsimd, "b_gqb", 16 * 8)
                        ph_end(gpsimd)

                    blk.sync(pb_sync)
                    blk.vector(pb_vector)
                    blk.scalar(pb_scalar)
                    blk.tensor(pb_tensor)
                    blk.gpsimd(pb_gpsimd)
                bar_k[0] += 1

            with ExitStack() as st2:
                xtpb = [st2.enter_context(
                    nc.sbuf_tensor(f"xtpb{i}", [128, 2048], BF16))
                    for i in range(2)]
                xbuf = [st2.enter_context(
                    nc.sbuf_tensor(f"xbuf{i}", [128, D], F32)) for i in range(2)]
                ybuf = [st2.enter_context(
                    nc.sbuf_tensor(f"ybuf{i}", [128, D], F32)) for i in range(2)]
                x1b = [st2.enter_context(
                    nc.sbuf_tensor(f"x1b{i}", [128, D], F32)) for i in range(2)]
                x1hb = [st2.enter_context(
                    nc.sbuf_tensor(f"x1hb{i}", [128, D], BF16)) for i in range(2)]
                x1lb = [st2.enter_context(
                    nc.sbuf_tensor(f"x1lb{i}", [128, D], BF16)) for i in range(2)]
                s1a = st2.enter_context(nc.sbuf_tensor("s1a", [128, NT], F32))
                s2a = st2.enter_context(nc.sbuf_tensor("s2a", [128, NT], F32))
                mucol = st2.enter_context(nc.sbuf_tensor("mucol", [128, 2], F32))
                vcol = st2.enter_context(nc.sbuf_tensor("vcol", [128, 2], F32))
                s0col = st2.enter_context(nc.sbuf_tensor("s0col", [128, 2], F32))
                tcol = st2.enter_context(nc.sbuf_tensor("tcol", [128, 2], F32))
                rcol = st2.enter_context(nc.sbuf_tensor("rcol", [128, 2], F32))
                nmcol = st2.enter_context(nc.sbuf_tensor("nmcol", [128, 2], F32))
                junk = st2.enter_context(nc.sbuf_tensor("junk1", [128, D], BF16))

                with (
                    nc.psum_tensor("c1_a0", [128, D], F32) as c1_a0,
                    nc.psum_tensor("c1_a1", [128, D], F32) as c1_a1,
                    nc.Block("pc1") as blk,
                ):
                    aps = [c1_a0, c1_a1]
                    passes = [(0, 0), (1, 0), (0, 1)]

                    def c1_sync(sync):
                        ph_start(sync)
                        for j in range(NT):
                            W(sync, "c_a", j - 1)
                            sync.dma_start(out=xtpb[j % 2][:], in_=xtp_s[j])\
                                .then_inc(sem[f"c_ldp{j % 2}"], 16)
                            W(sync, "c_y", j - 1)
                            sync.dma_start(out=xbuf[j % 2][:],
                                           in_=x_d[j * 128:(j + 1) * 128, :])\
                                .then_inc(sem[f"c_ldx{j % 2}"], 16)
                        ph_end(sync)

                    def c1_tensor(tensor):
                        ph_start(tensor)
                        for j in range(NT):
                            W(tensor, f"c_ldp{j % 2}", 16 * (j // 2 + 1))
                            W(tensor, "c_y", j - 1)
                            pb = xtpb[j % 2][:]
                            inst = None
                            for ip, (lh, rh) in enumerate(passes):
                                w = [wsum_hi, wsum_lo][rh]
                                for kc in range(DC):
                                    lhsT = pb[:, kc * 256 + lh * 128:
                                              kc * 256 + lh * 128 + 128]
                                    for nch in range(2):
                                        inst = tensor.matmul(
                                            aps[j % 2][:, nch * 512:
                                                       nch * 512 + 512],
                                            lhsT,
                                            w[:, kc * D + nch * 512:
                                              kc * D + nch * 512 + 512],
                                            start=(ip == 0 and kc == 0),
                                            stop=(ip == 2 and kc == DC - 1))
                            inst.then_inc(sem["c_a"], 1)
                        ph_end(tensor)

                    def c1_vector(vector):
                        ph_start(vector)
                        for jj in range(NT + 1):
                            if jj < NT:
                                j = jj
                                p = j % 2
                                W(vector, "c_a", j + 1)
                                W(vector, f"c_ldx{j % 2}", 16 * (j // 2 + 1))
                                W(vector, "c_s2", j - 1)
                                vector.scalar_tensor_tensor(
                                    ybuf[p][:], aps[p][:], 0.0, xbuf[p][:],
                                    OP.bypass, OP.add,
                                    accum_out=s1a[:, j:j + 1])\
                                    .then_inc(sem["c_y"], 1)
                            if jj >= 1:
                                i = jj - 1
                                p = i % 2
                                W(vector, "c_s2", i + 1)
                                vector.tensor_scalar(mucol[:, p:p + 1],
                                                     s1a[:, i:i + 1], 1.0 / D,
                                                     None, OP.mult)
                                vector.drain()
                                vector.tensor_tensor(tcol[:, p:p + 1],
                                                     mucol[:, p:p + 1],
                                                     mucol[:, p:p + 1], OP.mult)
                                vector.drain()
                                vector.scalar_tensor_tensor(
                                    vcol[:, p:p + 1], s2a[:, i:i + 1], 1.0 / D,
                                    tcol[:, p:p + 1], OP.mult, OP.subtract)
                                vector.drain()
                                vector.tensor_scalar(vcol[:, p:p + 1],
                                                     vcol[:, p:p + 1], LN_EPS,
                                                     None, OP.add)
                                vector.drain().then_inc(sem["c_v"], 1)
                                W(vector, "c_sq", i + 1)
                                vector.reciprocal(rcol[:, p:p + 1],
                                                  s0col[:, p:p + 1])
                                vector.drain()
                                vector.tensor_tensor(tcol[:, p:p + 1],
                                                     rcol[:, p:p + 1],
                                                     rcol[:, p:p + 1], OP.mult)
                                vector.drain()
                                vector.tensor_tensor(tcol[:, p:p + 1],
                                                     tcol[:, p:p + 1],
                                                     vcol[:, p:p + 1], OP.mult)
                                vector.drain()
                                vector.tensor_scalar(tcol[:, p:p + 1],
                                                     tcol[:, p:p + 1],
                                                     -0.5, 1.5, OP.mult, OP.add)
                                vector.drain()
                                vector.tensor_tensor(rcol[:, p:p + 1],
                                                     rcol[:, p:p + 1],
                                                     tcol[:, p:p + 1], OP.mult)
                                vector.drain()
                                vector.tensor_tensor(nmcol[:, p:p + 1],
                                                     mucol[:, p:p + 1],
                                                     rcol[:, p:p + 1], OP.mult)
                                vector.drain()
                                vector.tensor_scalar(nmcol[:, p:p + 1],
                                                     nmcol[:, p:p + 1], -1.0,
                                                     None, OP.mult)
                                vector.drain()
                                W(vector, f"c_st{i % 2}", 48 * (i // 2))
                                vector.tensor_scalar(x1b[p][:], ybuf[p][:],
                                                     rcol[:, p:p + 1],
                                                     nmcol[:, p:p + 1],
                                                     OP.mult, OP.add)
                                vector.tensor_copy(x1hb[p][:], x1b[p][:])
                                vector.tensor_sub(x1lb[p][:], x1b[p][:],
                                                  x1hb[p][:])\
                                    .then_inc(sem["c_x1"], 1)
                        ph_end(vector)

                    def c1_scalar(scalar):
                        ph_start(scalar)
                        for jj in range(NT + 1):
                            if jj < NT:
                                j = jj
                                W(scalar, "c_y", j + 1)
                                scalar.activation(junk[:], ybuf[j % 2][:],
                                                  AF.Square,
                                                  accum_out=s2a[:, j:j + 1])
                                scalar.drain().then_inc(sem["c_s2"], 1)
                            if jj >= 1:
                                i = jj - 1
                                W(scalar, "c_v", i + 1)
                                scalar.activation(s0col[:, i % 2:i % 2 + 1],
                                                  vcol[:, i % 2:i % 2 + 1],
                                                  AF.Sqrt)
                                scalar.drain().then_inc(sem["c_sq"], 1)
                        ph_end(scalar)

                    def c1_gpsimd(gpsimd):
                        ph_start(gpsimd)
                        for j in range(NT):
                            p = j % 2
                            W(gpsimd, "c_x1", j + 1)
                            r = slice(j * 128, (j + 1) * 128)
                            gpsimd.dma_start(out=x1_s[r, :], in_=x1b[p][:])\
                                .then_inc(sem[f"c_st{j % 2}"], 16)
                            gpsimd.dma_start(out=x1h_s[r, :], in_=x1hb[p][:])\
                                .then_inc(sem[f"c_st{j % 2}"], 16)
                            gpsimd.dma_start(out=x1l_s[r, :], in_=x1lb[p][:])\
                                .then_inc(sem[f"c_st{j % 2}"], 16)
                        W(gpsimd, "c_st0", 48 * (NT // 2))
                        W(gpsimd, "c_st1", 48 * (NT // 2))
                        ph_end(gpsimd)

                    blk.sync(c1_sync)
                    blk.tensor(c1_tensor)
                    blk.vector(c1_vector)
                    blk.scalar(c1_scalar)
                    blk.gpsimd(c1_gpsimd)
                bar_k[0] += 1

        with ExitStack() as st3:
            wfc_sb = st3.enter_context(
                nc.sbuf_tensor("wfc_sb", [128, DC * H], BF16))
            grpa = [st3.enter_context(
                nc.sbuf_tensor(f"grpa{i}", [128, DC * 2 * 512], BF16))
                for i in range(2)]
            gbuf = [st3.enter_context(
                nc.sbuf_tensor(f"gbuf{i}", [128, H], F32)) for i in range(2)]
            gpf = st3.enter_context(nc.sbuf_tensor("gpf", [128, H], F32))
            gph = [st3.enter_context(
                nc.sbuf_tensor(f"gph{i}", [128, H], BF16)) for i in range(2)]
            gpl = [st3.enter_context(
                nc.sbuf_tensor(f"gpl{i}", [128, H], BF16)) for i in range(2)]
            lnwb = st3.enter_context(nc.sbuf_tensor("lnwb", [128, H], F32))
            s1p = [st3.enter_context(
                nc.sbuf_tensor(f"s1p{i}", [128, 8], F32)) for i in range(2)]
            gtb = [st3.enter_context(
                nc.sbuf_tensor(f"gtb{i}", [128, 1024], BF16)) for i in range(2)]

            with (
                nc.psum_tensor("d_h0", [128, 1024], F32) as d_h0,
                nc.psum_tensor("d_h1", [128, 1024], F32) as d_h1,
                nc.psum_tensor("tr0", [128, 1024], BF16) as tr0,
                nc.psum_tensor("tr1", [128, 1024], BF16) as tr1,
                nc.Block("pc2a") as blk,
            ):
                d_hq = [d_h0, d_h1]
                trq = [tr0, tr1]
                def _xload_a(eng, g, hf):
                    rows = slice(g * 512, (g + 1) * 512)
                    src = (x1h_s, x1l_s)[hf]
                    sm = (f"d_ldT{g % 2}", f"d_ldT2{g % 2}")[hf]
                    for c in range(DC):
                        col = (c * 2 + hf) * 512
                        eng.dma_start(
                            out=grpa[g % 2][:, col:col + 512],
                            in_=src[rows, c * 128:(c + 1) * 128],
                            transpose=True,
                        ).then_inc(sem[sm], 16)

                def d_sync(sync):
                    ph_start(sync)
                    for kc in range(0, DC, 2):
                        sync.dma_start(out=wfc_sb[:, kc * H:(kc + 1) * H],
                                       in_=wfcq_s[kc])\
                            .then_inc(sem["d_wfc"], 16)
                    sync.dma_start(out=lnwb[0:1, :], in_=lnw_row_d[:, :])\
                        .then_inc(sem["d_lnw"], 16)
                    for g in range(NG_A):
                        W(sync, "d_hd", 16 * (g - 1))
                        _xload_a(sync, g, 0)
                        _xload_a(sync, g, 1)
                    ph_end(sync)

                def d_gpsimd(gpsimd):
                    ph_start(gpsimd)
                    for kc in range(1, DC, 2):
                        gpsimd.dma_start(out=wfc_sb[:, kc * H:(kc + 1) * H],
                                         in_=wfcq_s[kc])\
                            .then_inc(sem["d_wfcG"], 16)
                    W(gpsimd, "d_lnw", 16)
                    for k in range(7):
                        w = 1 << k
                        if k > 0:
                            W(gpsimd, "d_gst", 16 * k)
                        gpsimd.dma_start(out=lnwb[w:2 * w, :], in_=lnwb[0:w, :])\
                            .then_inc(sem["d_gst"], 16)
                    for j in range(NT):
                        for b in range(8):
                            i = 8 * j + b
                            W(gpsimd, "d_tev", i + 1)
                            gpsimd.dma_start(
                                out=gT_s[j, :, b * 1024:(b + 1) * 1024],
                                in_=gtb[i % 2][:])\
                                .then_inc(sem[f"d_st{i % 2}"], 16)
                    W(gpsimd, "d_st0", 16 * (8 * NT // 2))
                    W(gpsimd, "d_st1", 16 * (8 * NT // 2))
                    W(gpsimd, "d_gst", 16 * 7)
                    ph_end(gpsimd)

                def d_tensor(tensor):
                    ph_start(tensor)
                    for j in range(NT):
                        g, m = j // 4, j % 4
                        W(tensor, f"d_ldT{g % 2}", 16 * 8 * (g // 2 + 1))
                        W(tensor, f"d_ldT2{g % 2}", 16 * 8 * (g // 2 + 1))
                        if j == 0:
                            W(tensor, "d_wfc", 16 * 4)
                            W(tensor, "d_wfcG", 16 * 4)
                        for q in range(4):
                            iq = 4 * j + q
                            W(tensor, "d_gelu", iq - 1)
                            ps = d_hq[iq % 2]
                            inst = None
                            for kc in range(DC):
                                for hf in range(2):
                                    lhsT = grpa[g % 2][
                                        :, (kc * 2 + hf) * 512 + m * 128:
                                           (kc * 2 + hf) * 512 + m * 128 + 128]
                                    for nch in range(2):
                                        inst = tensor.matmul(
                                            ps[:, nch * 512:nch * 512 + 512],
                                            lhsT,
                                            wfc_sb[:, kc * H + q * 1024 +
                                                   nch * 512:
                                                   kc * H + q * 1024 +
                                                   nch * 512 + 512],
                                            start=(kc == 0 and hf == 0),
                                            stop=(kc == DC - 1 and hf == 1))
                            inst.then_inc(sem["d_hd"], 1)
                        if j >= 1:
                            tj = j - 1
                            W(tensor, "d_gp", tj + 1)
                            for b in range(8):
                                ib = 8 * tj + b
                                W(tensor, "d_tev", ib - 1)
                                inst = None
                                for sl in range(8):
                                    blkidx = b * 8 + sl
                                    jc, hf = blkidx // 2, blkidx % 2
                                    src = (gph, gpl)[hf][tj % 2][
                                        :, jc * 128:(jc + 1) * 128]
                                    inst = tensor.transpose(
                                        trq[ib % 2][:, sl * 128:(sl + 1) * 128],
                                        src, id_sb[:])
                                inst.then_inc(sem["d_trb"], 1)
                    tj = NT - 1
                    W(tensor, "d_gp", tj + 1)
                    for b in range(8):
                        ib = 8 * tj + b
                        W(tensor, "d_tev", ib - 1)
                        inst = None
                        for sl in range(8):
                            blkidx = b * 8 + sl
                            jc, hf = blkidx // 2, blkidx % 2
                            src = (gph, gpl)[hf][tj % 2][
                                :, jc * 128:(jc + 1) * 128]
                            inst = tensor.transpose(
                                trq[ib % 2][:, sl * 128:(sl + 1) * 128],
                                src, id_sb[:])
                        inst.then_inc(sem["d_trb"], 1)
                    ph_end(tensor)

                def _tevac(scalar, tj):
                    for b in range(8):
                        ib = 8 * tj + b
                        W(scalar, "d_trb", ib + 1)
                        W(scalar, f"d_st{ib % 2}", 16 * (ib // 2))
                        scalar.activation(gtb[ib % 2][:], trq[ib % 2][:],
                                          AF.Copy)\
                            .then_inc(sem["d_tev"], 1)

                def d_scalar(scalar):
                    ph_start(scalar)
                    for j in range(NT):
                        p = j % 2
                        for q in range(4):
                            iq = 4 * j + q
                            W(scalar, "d_hd", iq + 1)
                            if q == 0:
                                W(scalar, "d_gp", j - 1)
                            scalar.activation(
                                gbuf[p][:, q * 1024:(q + 1) * 1024],
                                d_hq[iq % 2][:],
                                AF.Gelu,
                                accum_out=s1p[p][:, q:q + 1])\
                                .then_inc(sem["d_gelu"], 1)
                        scalar.drain().then_inc(sem["d_acc"], 1)
                        if j >= 1:
                            _tevac(scalar, j - 1)
                    _tevac(scalar, NT - 1)
                    ph_end(scalar)

                def d_vector(vector):
                    ph_start(vector)
                    W(vector, "d_gst", 16 * 7)
                    for j in range(NT):
                        p = j % 2
                        W(vector, "d_acc", j + 1)
                        vector.reduce_sum(s1_t[:, j:j + 1],
                                          s1p[p][:, 0:4], axis=AX.X)
                        vector.scalar_tensor_tensor(
                            gpf[:], gbuf[p][:], 1.0, gbuf[p][:],
                            OP.bypass, OP.mult,
                            accum_out=s2_t[:, j:j + 1])
                        W(vector, "d_trb", 8 * (j - 1))
                        vector.tensor_tensor(gpf[:], gbuf[p][:], lnwb[:],
                                             OP.mult)
                        vector.tensor_copy(gph[p][:], gpf[:])
                        vector.tensor_sub(gpl[p][:], gpf[:], gph[p][:])\
                            .then_inc(sem["d_gp"], 1)
                    ph_end(vector)

                blk.sync(d_sync)
                blk.gpsimd(d_gpsimd)
                blk.tensor(d_tensor)
                blk.scalar(d_scalar)
                blk.vector(d_vector)
            bar_k[0] += 1

        with ExitStack() as st4:
            wpj_sb = st4.enter_context(
                nc.sbuf_tensor("wpj_sb", [128, HC * D], BF16))
            grpb = [st4.enter_context(
                nc.sbuf_tensor(f"grpb{i}", [128, HC * 2 * 256], BF16))
                for i in range(2)]
            x1db = [st4.enter_context(
                nc.sbuf_tensor(f"x1db{i}", [128, 2 * D], F32)) for i in range(2)]
            o1b = [st4.enter_context(
                nc.sbuf_tensor(f"o1b{i}", [128, D], F32)) for i in range(2)]
            o2b = st4.enter_context(nc.sbuf_tensor("o2b", [128, D], F32))
            x1cb = st4.enter_context(nc.sbuf_tensor("x1cb", [128, D], F32))
            y2b = [st4.enter_context(
                nc.sbuf_tensor(f"y2b{i}", [128, D], F32)) for i in range(2)]
            outb = [st4.enter_context(
                nc.sbuf_tensor(f"outb{i}", [128, D], F32)) for i in range(2)]
            c1b = st4.enter_context(nc.sbuf_tensor("c1b", [128, D], F32))
            c2b = st4.enter_context(nc.sbuf_tensor("c2b", [128, D], F32))
            r2a = st4.enter_context(nc.sbuf_tensor("r2a", [128, NT], F32))
            nm2a = st4.enter_context(nc.sbuf_tensor("nm2a", [128, NT], F32))
            tmpa = st4.enter_context(nc.sbuf_tensor("tmpa", [128, NT], F32))
            vea = st4.enter_context(nc.sbuf_tensor("vea", [128, NT], F32))
            s13 = st4.enter_context(nc.sbuf_tensor("s13", [128, 2], F32))
            s23 = st4.enter_context(nc.sbuf_tensor("s23", [128, 2], F32))
            mu3 = st4.enter_context(nc.sbuf_tensor("mu3", [128, 2], F32))
            v3 = st4.enter_context(nc.sbuf_tensor("v3", [128, 2], F32))
            s03 = st4.enter_context(nc.sbuf_tensor("s03", [128, 2], F32))
            t3 = st4.enter_context(nc.sbuf_tensor("t3", [128, 2], F32))
            r3 = st4.enter_context(nc.sbuf_tensor("r3", [128, 2], F32))
            nm3 = st4.enter_context(nc.sbuf_tensor("nm3", [128, 2], F32))
            junk = st4.enter_context(nc.sbuf_tensor("junk3", [128, D], BF16))

            with (
                nc.psum_tensor("e_o0", [128, D], F32) as e_o0,
                nc.psum_tensor("e_o1p", [128, D], F32) as e_o1p,
                nc.Block("pc2b") as blk,
            ):
                ops_ = [e_o0, e_o1p]

                def _gload_b(eng, g):
                    gv = grpb[g % 2][:].rearrange(
                        "p (c mm tt) -> p c mm tt", mm=2, tt=128)
                    for m in range(2):
                        eng.dma_start(
                            out=gv[:, :, m, :],
                            in_=gT_s[2 * g + m],
                        ).then_inc(sem[f"e_ldT{g % 2}"], 16)

                def e_sync(sync):
                    ph_start(sync)
                    for jc in range(0, HC, 2):
                        sync.dma_start(out=wpj_sb[:, jc * D:(jc + 1) * D],
                                       in_=wprojq_s[jc])\
                            .then_inc(sem["e_wpj"], 16)
                    for g in range(NG_B):
                        W(sync, "e_mm", 2 * (g - 1))
                        _gload_b(sync, g)
                        W(sync, "e_y2", 2 * (g - 1))
                        sync.dma_start(
                            out=x1db[g % 2][:].rearrange("p (a d) -> p a d", a=2),
                            in_=x1_s[g * 256:(g + 1) * 256, :].rearrange(
                                "(a p) d -> p a d", p=128),
                        ).then_inc(sem[f"e_ldx{g % 2}"], 16)
                    ph_end(sync)

                def e_vector(vector):
                    ph_start(vector)
                    vector.tensor_scalar(tmpa[:], s1_t[:], 1.0 / H, None,
                                         OP.mult)
                    vector.drain()
                    vector.tensor_tensor(vea[:], tmpa[:], tmpa[:], OP.mult)
                    vector.drain()
                    vector.scalar_tensor_tensor(vea[:], s2_t[:], 1.0 / H,
                                                vea[:], OP.mult, OP.subtract)
                    vector.drain()
                    vector.tensor_scalar(vea[:], vea[:], LN_EPS, None, OP.add)
                    vector.drain().then_inc(sem["e_r2"], 1)
                    W(vector, "e_r2", 2)
                    vector.reciprocal(r2a[:], nm2a[:])
                    vector.drain()
                    vector.tensor_tensor(nm2a[:], r2a[:], r2a[:], OP.mult)
                    vector.drain()
                    vector.tensor_tensor(nm2a[:], nm2a[:], vea[:], OP.mult)
                    vector.drain()
                    vector.tensor_scalar(nm2a[:], nm2a[:], -0.5, 1.5,
                                         OP.mult, OP.add)
                    vector.drain()
                    vector.tensor_tensor(r2a[:], r2a[:], nm2a[:], OP.mult)
                    vector.drain()
                    vector.tensor_tensor(nm2a[:], tmpa[:], r2a[:], OP.mult)
                    vector.drain()
                    vector.tensor_scalar(nm2a[:], nm2a[:], -1.0, None, OP.mult)
                    vector.drain().then_inc(sem["e_r2"], 1)
                    W(vector, "e_c12b", 16 * 16)
                    for jj in range(NT + 2):
                        if jj >= 2:
                            i2 = jj - 2
                            p = i2 % 2
                            W(vector, "e_sq3", i2 + 1)
                            vector.reciprocal(r3[:, p:p + 1], s03[:, p:p + 1])
                            vector.drain()
                            vector.tensor_tensor(t3[:, p:p + 1], r3[:, p:p + 1],
                                                 r3[:, p:p + 1], OP.mult)
                            vector.drain()
                            vector.tensor_tensor(t3[:, p:p + 1], t3[:, p:p + 1],
                                                 v3[:, p:p + 1], OP.mult)
                            vector.drain()
                            vector.tensor_scalar(t3[:, p:p + 1], t3[:, p:p + 1],
                                                 -0.5, 1.5, OP.mult, OP.add)
                            vector.drain()
                            vector.tensor_tensor(r3[:, p:p + 1], r3[:, p:p + 1],
                                                 t3[:, p:p + 1], OP.mult)
                            vector.drain()
                            vector.tensor_tensor(nm3[:, p:p + 1],
                                                 mu3[:, p:p + 1],
                                                 r3[:, p:p + 1], OP.mult)
                            vector.drain()
                            vector.tensor_scalar(nm3[:, p:p + 1],
                                                 nm3[:, p:p + 1], -1.0,
                                                 None, OP.mult)
                            vector.drain()
                            W(vector, f"e_st{i2 % 2}", 16 * (i2 // 2))
                            vector.tensor_scalar(outb[p][:], y2b[p][:],
                                                 r3[:, p:p + 1],
                                                 nm3[:, p:p + 1],
                                                 OP.mult, OP.add)\
                                .then_inc(sem["e_out"], 1)
                        if 1 <= jj <= NT:
                            i1 = jj - 1
                            p = i1 % 2
                            W(vector, "e_s23", i1 + 1)
                            W(vector, "e_sq3", i1 - 1)
                            vector.tensor_scalar(mu3[:, p:p + 1],
                                                 s13[:, p:p + 1], 1.0 / D,
                                                 None, OP.mult)
                            vector.drain()
                            vector.tensor_tensor(t3[:, p:p + 1],
                                                 mu3[:, p:p + 1],
                                                 mu3[:, p:p + 1], OP.mult)
                            vector.drain()
                            vector.scalar_tensor_tensor(
                                v3[:, p:p + 1], s23[:, p:p + 1], 1.0 / D,
                                t3[:, p:p + 1], OP.mult, OP.subtract)
                            vector.drain()
                            vector.tensor_scalar(v3[:, p:p + 1], v3[:, p:p + 1],
                                                 LN_EPS, None, OP.add)
                            vector.drain().then_inc(sem["e_v3"], 1)
                        if jj < NT:
                            j = jj
                            p = j % 2
                            g = j // 2
                            W(vector, "e_o1", j + 1)
                            W(vector, f"e_ldx{g % 2}", 16 * (g // 2 + 1))
                            vector.scalar_tensor_tensor(
                                o2b[:], c1b[:], nm2a[:, j:j + 1], o1b[p][:],
                                OP.mult, OP.add)
                            vector.tensor_tensor(
                                x1cb[:], x1db[g % 2][:, (j % 2) * D:
                                                     (j % 2) * D + D],
                                c2b[:], OP.add)
                            W(vector, "e_s23", j - 1)
                            vector.scalar_tensor_tensor(
                                y2b[p][:], o2b[:], 0.0, x1cb[:],
                                OP.bypass, OP.add,
                                accum_out=s13[:, p:p + 1])\
                                .then_inc(sem["e_y2"], 1)
                    ph_end(vector)

                def e_scalar(scalar):
                    ph_start(scalar)
                    W(scalar, "e_r2", 1)
                    scalar.activation(nm2a[:], vea[:], AF.Sqrt)
                    scalar.drain().then_inc(sem["e_r2"], 1)
                    for jj in range(NT + 2):
                        if jj < NT:
                            j = jj
                            p = j % 2
                            W(scalar, "e_mm", j + 1)
                            W(scalar, "e_r2", 3)
                            W(scalar, "e_y2", j - 1)
                            scalar.activation(o1b[p][:], ops_[p][:], AF.Copy,
                                              scale=r2a[:, j:j + 1])\
                                .then_inc(sem["e_o1"], 1)
                        if 1 <= jj <= NT:
                            i = jj - 1
                            W(scalar, "e_y2", i + 1)
                            W(scalar, "e_v3", i - 1)
                            scalar.activation(junk[:], y2b[i % 2][:],
                                              AF.Square,
                                              accum_out=s23[:, i % 2:i % 2 + 1])
                            scalar.drain().then_inc(sem["e_s23"], 1)
                        if jj >= 2:
                            i2 = jj - 2
                            W(scalar, "e_v3", i2 + 1)
                            W(scalar, "e_out", i2 - 1)
                            scalar.activation(s03[:, i2 % 2:i2 % 2 + 1],
                                              v3[:, i2 % 2:i2 % 2 + 1],
                                              AF.Sqrt)
                            scalar.drain().then_inc(sem["e_sq3"], 1)
                    ph_end(scalar)

                def e_tensor(tensor):
                    ph_start(tensor)
                    for j in range(NT):
                        g, m = j // 2, j % 2
                        W(tensor, f"e_ldT{g % 2}", 16 * 2 * (g // 2 + 1))
                        W(tensor, "e_o1", j - 1)
                        if j == 0:
                            W(tensor, "e_wpj", 16 * 16)
                            W(tensor, "e_wpjG", 16 * 16)
                        inst = None
                        for jc in range(HC):
                            for hf in range(2):
                                lhsT = grpb[g % 2][
                                    :, (jc * 2 + hf) * 256 + m * 128:
                                       (jc * 2 + hf) * 256 + m * 128 + 128]
                                for nch in range(2):
                                    inst = tensor.matmul(
                                        ops_[j % 2][:, nch * 512:nch * 512 + 512],
                                        lhsT,
                                        wpj_sb[:, jc * D + nch * 512:
                                               jc * D + nch * 512 + 512],
                                        start=(jc == 0 and hf == 0),
                                        stop=(jc == HC - 1 and hf == 1))
                        inst.then_inc(sem["e_mm"], 1)
                    ph_end(tensor)

                def e_gpsimd(gpsimd):
                    ph_start(gpsimd)
                    for jc in range(1, HC, 2):
                        gpsimd.dma_start(out=wpj_sb[:, jc * D:(jc + 1) * D],
                                         in_=wprojq_s[jc])\
                            .then_inc(sem["e_wpjG"], 16)
                    gpsimd.dma_start(out=c1b[0:1, :], in_=c12_sb[0:1, :])\
                        .then_inc(sem["e_c12b"], 16)
                    gpsimd.dma_start(out=c2b[0:1, :], in_=c12_sb[1:2, :])\
                        .then_inc(sem["e_c12b"], 16)
                    for k in range(6):
                        w = 1 << k
                        W(gpsimd, "e_c12b", 16 * 2 * (k + 1))
                        gpsimd.dma_start(out=c1b[w:2 * w, :], in_=c1b[0:w, :])\
                            .then_inc(sem["e_c12b"], 16)
                        gpsimd.dma_start(out=c2b[w:2 * w, :], in_=c2b[0:w, :])\
                            .then_inc(sem["e_c12b"], 16)
                    W(gpsimd, "e_c12b", 16 * 14)
                    gpsimd.dma_start(out=c1b[64:128, :], in_=c1b[0:64, :])\
                        .then_inc(sem["e_c12b"], 16)
                    gpsimd.dma_start(out=c2b[64:128, :], in_=c2b[0:64, :])\
                        .then_inc(sem["e_c12b"], 16)
                    for j in range(NT):
                        W(gpsimd, "e_out", j + 1)
                        gpsimd.dma_start(out=out_d[j * 128:(j + 1) * 128, :],
                                         in_=outb[j % 2][:])\
                            .then_inc(sem[f"e_st{j % 2}"], 16)
                    ndbg = 0
                    if dbg:
                        for dsts in [(dbg_d[0:1, :], gq_sb[:]),
                                     (dbg_d[1:2, :], c12_sb[0:1, :]),
                                     (dbg_d[2:3, :], c12_sb[1:2, :]),
                                     (dbg2_d[:, 0:NT], s1_t[:]),
                                     (dbg2_d[:, NT:2 * NT], s2_t[:]),
                                     (dbg2_d[:, 2 * NT:3 * NT], r2a[:]),
                                     (dbg2_d[:, 3 * NT:4 * NT], nm2a[:])]:
                            gpsimd.dma_start(out=dsts[0], in_=dsts[1])\
                                .then_inc(sem["e_std"], 16)
                            ndbg += 1
                    W(gpsimd, "e_st0", 16 * (NT // 2))
                    W(gpsimd, "e_st1", 16 * (NT // 2))
                    if ndbg:
                        W(gpsimd, "e_std", 16 * ndbg)
                    ph_end(gpsimd)

                blk.sync(e_sync)
                blk.vector(e_vector)
                blk.scalar(e_scalar)
                blk.tensor(e_tensor)
                blk.gpsimd(e_gpsimd)
            bar_k[0] += 1

    return nc


def kernel(L0_attn_x, L0_attn_Wq, L0_attn_Wk, L0_attn_Wv, L0_attn_Wr,
           L0_ffn_W_fc, L0_ffn_ln_w, L0_ffn_ln_b, L0_ffn_W_proj):
    B, S, D_ = L0_attn_x.shape
    assert D_ == D
    x = np.ascontiguousarray(np.asarray(L0_attn_x, dtype=np.float32))
    f32 = lambda a: np.ascontiguousarray(np.asarray(a, dtype=np.float32))
    wq = f32(L0_attn_Wq)
    wk = f32(L0_attn_Wk)
    wvT = np.ascontiguousarray(f32(L0_attn_Wv).T)
    wr = f32(L0_attn_Wr)
    wfcT = np.ascontiguousarray(f32(L0_ffn_W_fc).T)
    wprojT = np.ascontiguousarray(f32(L0_ffn_W_proj).T)
    lnw = f32(L0_ffn_ln_w).reshape(-1)
    lnb = f32(L0_ffn_ln_b).reshape(-1)
    lnw_row = np.ascontiguousarray(lnw.reshape(1, H))
    lnwT = np.ascontiguousarray(lnw.reshape(HC, 128).T)
    lnbT = np.ascontiguousarray(lnb.reshape(HC, 128).T)

    nc = build(S)
    in_maps = []
    for i in range(B):
        in_maps.append({
            "x": np.ascontiguousarray(x[i]),
            "xT": np.ascontiguousarray(x[i].T),
            "wq": wq, "wk": wk, "wvT": wvT, "wr": wr,
            "wfcT": wfcT, "wprojT": wprojT,
            "lnw_row": lnw_row, "lnwT": lnwT, "lnbT": lnbT,
        })
    res = run_bass_kernel_spmd(nc, in_maps, core_ids=list(range(B)))
    out = np.stack([res.results[i]["out"] for i in range(B)], axis=0)
    return out.astype(np.float32)


# version 08
# speedup vs baseline: 1.1138x; 1.0587x over previous
import numpy as np
from contextlib import ExitStack

import concourse.bass as bass
import concourse.mybir as mybir
from concourse.bass_utils import run_bass_kernel_spmd
from concourse.alu_op_type import AluOpType as OP

F32 = mybir.dt.float32
F32R = mybir.dt.float32r
BF16 = mybir.dt.bfloat16
AF = mybir.ActivationFunctionType
AX = mybir.AxisListType

D = 1024
H = 4096
DC = D // 128
HC = H // 128
SCALE = 1.0 / 32.0
LN_EPS = 1e-5
Q_EPS = 1e-6


def build(S=8192, dbg=False):
    NT = S // 128
    NG_A = NT // 4
    NG_B = NT // 2
    nc = bass.Bass("TRN2", target_bir_lowering=False, debug=False,
                   detect_race_conditions=False)

    x_d = nc.dram_tensor("x", [S, D], F32, kind="ExternalInput").ap()
    xT_d = nc.dram_tensor("xT", [D, S], F32, kind="ExternalInput").ap()
    wq_d = nc.dram_tensor("wq", [D, D], F32, kind="ExternalInput").ap()
    wk_d = nc.dram_tensor("wk", [D, D], F32, kind="ExternalInput").ap()
    wvT_d = nc.dram_tensor("wvT", [D, D], F32, kind="ExternalInput").ap()
    wr_d = nc.dram_tensor("wr", [D, D], F32, kind="ExternalInput").ap()
    wfcT_d = nc.dram_tensor("wfcT", [D, H], F32, kind="ExternalInput").ap()
    wprojT_d = nc.dram_tensor("wprojT", [H, D], F32, kind="ExternalInput").ap()
    lnw_row_d = nc.dram_tensor("lnw_row", [1, H], F32, kind="ExternalInput").ap()
    lnwT_d = nc.dram_tensor("lnwT", [128, HC], F32, kind="ExternalInput").ap()
    lnbT_d = nc.dram_tensor("lnbT", [128, HC], F32, kind="ExternalInput").ap()
    out_d = nc.dram_tensor("out", [S, D], F32, kind="ExternalOutput").ap()

    KI = "ExternalOutput" if dbg else "Internal"
    k_s = nc.dram_tensor("k_s", [S, D], F32, kind=KI).ap()
    xtp_s = nc.dram_tensor("xtp_s", [NT, 128, 2048], BF16, kind=KI).ap()
    x1_s = nc.dram_tensor("x1_s", [S, D], F32, kind=KI).ap()
    x1h_s = nc.dram_tensor("x1h_s", [S, D], BF16, kind=KI).ap()
    x1l_s = nc.dram_tensor("x1l_s", [S, D], BF16, kind=KI).ap()
    gT_s = nc.dram_tensor("gT_s", [NT, 128, HC * 2 * 128], BF16, kind=KI).ap()
    wfcq_s = nc.dram_tensor("wfcq_s", [DC, 128, H], BF16, kind=KI).ap()
    wprojq_s = nc.dram_tensor("wprojq_s", [HC, 128, D], BF16, kind=KI).ap()
    if dbg:
        dbg_d = nc.dram_tensor("dbg", [4, D], F32, kind="ExternalOutput").ap()
        dbg2_d = nc.dram_tensor("dbg2", [128, 4 * NT], F32,
                                kind="ExternalOutput").ap()

    xT_r = xT_d.rearrange("(c p) s -> p c s", p=128)

    es = ExitStack()
    with es:
        sem = {}
        for name in [
            "bar",
            "p_ld0", "p_ld1", "p_cons", "p_gr", "p_bc", "p_qt",
            "p_qc0", "p_qc1", "p_mm", "p_lnld", "p_lnp", "p_spl",
            "a_ldx0", "a_ldx1", "a_split", "a_qk", "a_exp", "a_w2", "a_gq",
            "a_kev", "a_pst0", "a_pst1", "a_kst0", "a_kst1", "a_gqev",
            "b_ldk0", "b_ldk1", "b_p", "b_e", "b_w2", "b_gk", "b_gkt",
            "b_wld0", "b_wld1", "b_wsplit", "b_wc", "b_wsum", "b_gqb",
            "c_ldp0", "c_ldp1", "c_ldx0", "c_ldx1", "c_a", "c_y", "c_s2",
            "c_v", "c_sq", "c_x1", "c_st0", "c_st1",
            "d_wfc", "d_wfcG", "d_lnw", "d_ldT0", "d_ldT1", "d_ldT20",
            "d_ldT21", "d_hd", "d_gelu", "d_gp", "d_gst", "d_acc", "d_trb", "d_tev", "d_st0", "d_st1",
            "e_wpj", "e_wpjG", "e_ldT0", "e_ldT1", "e_ldT20", "e_ldT21",
            "e_ldx0", "e_ldx1", "e_r2", "e_mm", "e_o1",
            "e_y2", "e_s23", "e_v3", "e_sq3", "e_out", "e_st0", "e_st1",
            "e_std", "e_c12b",
        ]:
            sem[name] = es.enter_context(nc.semaphore(name))

        BAR = sem["bar"]
        bar_k = [0]

        def ph_start(eng):
            if bar_k[0] > 0:
                eng.wait_ge(BAR, 5 * bar_k[0])

        def ph_end(eng):
            eng.drain().then_inc(BAR, 1)

        def W(eng, s, v):
            if v > 0:
                eng.wait_ge(sem[s], v)

        ones_r = es.enter_context(nc.sbuf_tensor("ones_r", [128, 2], F32R))
        ones_f = es.enter_context(nc.sbuf_tensor("ones_f", [128, 2], F32))
        gq_sb = es.enter_context(nc.sbuf_tensor("gq_sb", [1, D], F32))
        c12_sb = es.enter_context(nc.sbuf_tensor("c12_sb", [2, D], F32))
        s1_t = es.enter_context(nc.sbuf_tensor("s1_t", [128, NT], F32))
        id_sb = es.enter_context(nc.sbuf_tensor("id_sb", [128, 128], BF16))
        s2_t = es.enter_context(nc.sbuf_tensor("s2_t", [128, NT], F32))

        with ExitStack() as st0:
            wq_hi = st0.enter_context(nc.sbuf_tensor("wq_hi", [128, D * DC], BF16))
            wq_lo = st0.enter_context(nc.sbuf_tensor("wq_lo", [128, D * DC], BF16))
            wk_hi = st0.enter_context(nc.sbuf_tensor("wk_hi", [128, D * DC], BF16))
            wk_lo = st0.enter_context(nc.sbuf_tensor("wk_lo", [128, D * DC], BF16))
            wtile = [st0.enter_context(nc.sbuf_tensor(f"wtile{i}", [128, H], F32))
                     for i in range(2)]
            qtile = [st0.enter_context(nc.sbuf_tensor(f"qtile{i}", [128, H], BF16))
                     for i in range(2)]
            gcol = st0.enter_context(nc.sbuf_tensor("gcol", [128, 40], F32))
            gred = st0.enter_context(nc.sbuf_tensor("gred", [128, 1], F32))
            gtot = st0.enter_context(nc.sbuf_tensor("gtot", [1, 128], F32))
            thrT = st0.enter_context(nc.sbuf_tensor("thrT", [1, 4], F32))
            thr_c = st0.enter_context(nc.sbuf_tensor("thr_c", [128, 2], F32))
            lnpair = st0.enter_context(nc.sbuf_tensor("lnpair", [128, 4 * HC], BF16))
            lnT_f = st0.enter_context(nc.sbuf_tensor("lnT_f", [128, 2 * HC], F32))

            def tile_src(t):
                if t < 8:
                    return wfcT_d[(t % 8) * 128:(t % 8) * 128 + 128, :], H
                if t < 16:
                    return wfcT_d[(t % 8) * 128:(t % 8) * 128 + 128, :], H
                if t < 48:
                    i = t - 16
                    return wprojT_d[i * 128:(i + 1) * 128, :], D
                if t < 80:
                    i = t - 48
                    return wprojT_d[i * 128:(i + 1) * 128, :], D
                if t < 88:
                    i = t - 80
                    return wq_d[i * 128:(i + 1) * 128, :], D
                i = t - 88
                return wk_d[i * 128:(i + 1) * 128, :], D

            with (
                nc.psum_tensor("p0_c12", [2, D], F32) as p0_c12,
                nc.Block("p0") as blk,
            ):
                def p0_sync(sync):
                    ph_start(sync)
                    sync.dma_start(out=lnT_f[:, 0:HC], in_=lnwT_d[:, :])\
                        .then_inc(sem["p_lnld"], 16)
                    sync.dma_start(out=lnT_f[:, HC:2 * HC], in_=lnbT_d[:, :])\
                        .then_inc(sem["p_lnld"], 16)
                    for t in range(96):
                        if t < 82:
                            W(sync, "p_cons", t - 1)
                        else:
                            W(sync, "p_spl", t - 81)
                        src, w = tile_src(t)
                        sync.dma_start(out=wtile[t % 2][:, 0:w], in_=src)\
                            .then_inc(sem[f"p_ld{t % 2}"], 16)
                    ph_end(sync)

                def quant_ops(vector, t, w, qk):
                    W(vector, f"p_qc{qk % 2}", 16 * (qk // 2))
                    if qk - 2 >= 8:
                        W(vector, "p_mm", qk - 9)
                    vector.tensor_scalar(qtile[t % 2][:, 0:w],
                                         wtile[t % 2][:, 0:w],
                                         thr_c[:, 0:1], None, OP.is_gt)
                    vector.scalar_tensor_tensor(
                        qtile[t % 2][:, 0:w], wtile[t % 2][:, 0:w],
                        thr_c[:, 1:2], qtile[t % 2][:, 0:w],
                        OP.is_lt, OP.subtract)\
                        .then_inc(sem["p_cons"], 1)
                    vector.tensor_scalar(qtile[t % 2][:, 0:w],
                                         qtile[t % 2][:, 0:w],
                                         -1.0, None, OP.mult)\
                        .then_inc(sem["p_qt"], 1)

                def p0_vector(vector):
                    ph_start(vector)
                    vector.memset(ones_f[:], 1.0)
                    vector.tensor_copy(ones_r[:], ones_f[:])
                    for i in range(8):
                        W(vector, f"p_ld{i % 2}", 16 * (i // 2 + 1))
                        vector.reduce_sum(gcol[:, i:i + 1], wtile[i % 2][:],
                                          axis=AX.X, apply_absolute_value=True)\
                            .then_inc(sem["p_cons"], 1)
                    vector.drain()
                    vector.reduce_sum(gred[:], gcol[:, 0:8], axis=AX.X)\
                        .then_inc(sem["p_gr"], 1)
                    W(vector, "p_bc", 16)
                    vector.reduce_sum(thrT[:, 0:1], gtot[:], axis=AX.X)
                    vector.drain()
                    vector.tensor_scalar(thrT[:, 0:1], thrT[:, 0:1],
                                         1.0 / (D * H), None, OP.mult)
                    vector.drain()
                    vector.tensor_scalar(thrT[:, 0:1], thrT[:, 0:1], Q_EPS, 0.5,
                                         OP.add, OP.mult)
                    vector.drain().then_inc(sem["p_gr"], 1)
                    W(vector, "p_bc", 16 * 9)
                    vector.tensor_scalar(thr_c[:, 1:2], thr_c[:, 0:1], -1.0,
                                         None, OP.mult)
                    vector.drain()
                    for i in range(8):
                        t = 8 + i
                        W(vector, f"p_ld{t % 2}", 16 * (t // 2 + 1))
                        quant_ops(vector, t, H, i)
                    for i in range(32):
                        t = 16 + i
                        W(vector, f"p_ld{t % 2}", 16 * (t // 2 + 1))
                        vector.reduce_sum(gcol[:, 8 + i:9 + i],
                                          wtile[t % 2][:, 0:D],
                                          axis=AX.X, apply_absolute_value=True)\
                            .then_inc(sem["p_cons"], 1)
                    vector.drain()
                    vector.reduce_sum(gred[:], gcol[:, 8:40], axis=AX.X)\
                        .then_inc(sem["p_gr"], 1)
                    W(vector, "p_bc", 16 * 10)
                    vector.reduce_sum(thrT[:, 2:3], gtot[:], axis=AX.X)
                    vector.drain()
                    vector.tensor_scalar(thrT[:, 2:3], thrT[:, 2:3],
                                         1.0 / (D * H), None, OP.mult)
                    vector.drain()
                    vector.tensor_scalar(thrT[:, 2:3], thrT[:, 2:3], Q_EPS, 0.5,
                                         OP.add, OP.mult)
                    vector.drain().then_inc(sem["p_gr"], 1)
                    W(vector, "p_bc", 16 * 18)
                    vector.tensor_scalar(thr_c[:, 1:2], thr_c[:, 0:1], -1.0,
                                         None, OP.mult)
                    vector.drain()
                    W(vector, "p_lnld", 32)
                    pv = lnpair[:].rearrange("p (h c s) -> p h c s", h=2, s=2)
                    vector.tensor_copy(pv[:, 0, :, 0], lnT_f[:, 0:HC])
                    vector.tensor_copy(pv[:, 0, :, 1], lnT_f[:, HC:2 * HC])
                    vector.tensor_sub(pv[:, 1, :, 0], lnT_f[:, 0:HC],
                                      pv[:, 0, :, 0])
                    vector.tensor_sub(pv[:, 1, :, 1], lnT_f[:, HC:2 * HC],
                                      pv[:, 0, :, 1])
                    vector.drain().then_inc(sem["p_lnp"], 1)
                    for i in range(32):
                        t = 48 + i
                        W(vector, f"p_ld{t % 2}", 16 * (t // 2 + 1))
                        quant_ops(vector, t, D, 8 + i)
                    for i in range(16):
                        t = 80 + i
                        W(vector, f"p_ld{t % 2}", 16 * (t // 2 + 1))
                        hi = wq_hi if i < 8 else wk_hi
                        lo = wq_lo if i < 8 else wk_lo
                        c = (i % 8) * D
                        vector.tensor_copy(hi[:, c:c + D], wtile[t % 2][:, 0:D])
                        vector.tensor_sub(lo[:, c:c + D], wtile[t % 2][:, 0:D],
                                          hi[:, c:c + D])\
                            .then_inc(sem["p_spl"], 1)
                    W(vector, "p_mm", 32)
                    vector.tensor_copy(c12_sb[:], p0_c12[:])
                    ph_end(vector)

                def p0_tensor(tensor):
                    ph_start(tensor)
                    W(tensor, "p_lnp", 1)
                    for i in range(32):
                        W(tensor, "p_qt", 8 + i + 1)
                        t = 48 + i
                        inst = None
                        for half in range(2):
                            c = (half * HC + i) * 2
                            for nch in range(2):
                                cs = slice(nch * 512, nch * 512 + 512)
                                inst = tensor.matmul(
                                    p0_c12[:, cs], lnpair[:, c:c + 2],
                                    qtile[t % 2][:, 0:D][:, cs],
                                    start=(i == 0 and half == 0),
                                    stop=(i == 31 and half == 1))
                        inst.then_inc(sem["p_mm"], 1)
                    ph_end(tensor)

                def p0_scalar(scalar):
                    ph_start(scalar)
                    ph_end(scalar)

                def p0_gpsimd(gpsimd):
                    ph_start(gpsimd)
                    gpsimd.memset(id_sb[:], 0.0)
                    gpsimd.affine_select(
                        out=id_sb[:], in_=id_sb[:],
                        compare_op=OP.not_equal, fill=1.0, base=0,
                        pattern=[[-1, 128]], channel_multiplier=1)

                    def tree(dst, srcrow, base):
                        gpsimd.dma_start(out=dst[0:1, 0:1], in_=srcrow)\
                            .then_inc(sem["p_bc"], 16)
                        for k in range(6):
                            w = 1 << k
                            W(gpsimd, "p_bc", base + 16 * (k + 1))
                            gpsimd.dma_start(out=dst[w:2 * w, 0:1],
                                             in_=dst[0:w, 0:1])\
                                .then_inc(sem["p_bc"], 16)
                        W(gpsimd, "p_bc", base + 16 * 7)
                        gpsimd.dma_start(out=dst[64:128, 0:1],
                                         in_=dst[0:64, 0:1])\
                            .then_inc(sem["p_bc"], 16)

                    W(gpsimd, "p_gr", 1)
                    gpsimd.dma_start(out=gtot[0:1, :], in_=gred[:, 0:1])\
                        .then_inc(sem["p_bc"], 16)
                    W(gpsimd, "p_gr", 2)
                    tree(thr_c, thrT[0:1, 0:1], 16)
                    for i in range(8):
                        W(gpsimd, "p_qt", i + 1)
                        gpsimd.dma_start(out=wfcq_s[i],
                                         in_=qtile[i % 2][:])\
                            .then_inc(sem[f"p_qc{i % 2}"], 16)
                    W(gpsimd, "p_gr", 3)
                    gpsimd.dma_start(out=gtot[0:1, :], in_=gred[:, 0:1])\
                        .then_inc(sem["p_bc"], 16)
                    W(gpsimd, "p_gr", 4)
                    tree(thr_c, thrT[0:1, 2:3], 160)
                    for i in range(32):
                        W(gpsimd, "p_qt", 8 + i + 1)
                        gpsimd.dma_start(out=wprojq_s[i],
                                         in_=qtile[i % 2][:, 0:D])\
                            .then_inc(sem[f"p_qc{i % 2}"], 16)
                    W(gpsimd, "p_qc0", 16 * 20)
                    W(gpsimd, "p_qc1", 16 * 20)
                    ph_end(gpsimd)

                blk.sync(p0_sync)
                blk.vector(p0_vector)
                blk.tensor(p0_tensor)
                blk.scalar(p0_scalar)
                blk.gpsimd(p0_gpsimd)
            bar_k[0] += 1

            xtile = [st0.enter_context(nc.sbuf_tensor(f"xtile{i}", [128, D], F32))
                     for i in range(2)]
            pairb = [st0.enter_context(nc.sbuf_tensor(f"pairb{i}", [128, 2048], BF16))
                     for i in range(2)]
            ebuf = [st0.enter_context(nc.sbuf_tensor(f"ebuf{i}", [128, D], F32))
                    for i in range(2)]
            dcol = st0.enter_context(nc.sbuf_tensor("dcol", [128, 2], F32))
            rdcol = st0.enter_context(nc.sbuf_tensor("rdcol", [128, 2], F32))
            w2b = [st0.enter_context(nc.sbuf_tensor(f"w2b{i}", [128, D], F32R))
                   for i in range(2)]
            kev = [st0.enter_context(nc.sbuf_tensor(f"kev{i}", [128, D], F32))
                   for i in range(2)]

            with (
                nc.psum_tensor("pa_q0", [128, D], F32) as pa_q0,
                nc.psum_tensor("pa_q1", [128, D], F32) as pa_q1,
                nc.psum_tensor("pa_k", [128, D], F32) as pa_k,
                nc.psum_tensor("pa_gq", [1, D], F32) as pa_gq,
                nc.Block("pa") as blk,
            ):
                qps = [pa_q0, pa_q1]
                passes = [(0, 0), (1, 0), (0, 1)]

                def pa_sync(sync):
                    ph_start(sync)
                    for j in range(NT):
                        W(sync, "a_split", 2 * (j - 1))
                        sync.dma_start(
                            out=xtile[j % 2][:].rearrange("p (c t) -> p c t", c=DC),
                            in_=xT_r[:, :, j * 128:(j + 1) * 128],
                        ).then_inc(sem[f"a_ldx{j % 2}"], 16)
                    ph_end(sync)

                def pa_vector(vector):
                    ph_start(vector)
                    for jj in range(NT + 1):
                        if jj < NT:
                            j = jj
                            W(vector, f"a_ldx{j % 2}", 16 * (j // 2 + 1))
                            W(vector, "a_qk", 2 * (j - 1))
                            W(vector, f"a_pst{j % 2}", 16 * (j // 2))
                            dst = pairb[j % 2][:].rearrange(
                                "p (c h t) -> p c h t", c=DC, h=2)
                            src = xtile[j % 2][:].rearrange(
                                "p (c t) -> p c t", c=DC)
                            vector.tensor_copy(dst[:, :, 0, :], src)
                            vector.tensor_tensor(dst[:, :, 1, :], src,
                                                 dst[:, :, 0, :], OP.subtract)\
                                .then_inc(sem["a_split"], 2)
                        if jj >= 1:
                            i = jj - 1
                            p = i % 2
                            W(vector, "a_exp", i + 1)
                            vector.reciprocal(rdcol[:, p:p + 1], dcol[:, p:p + 1])
                            vector.drain()
                            W(vector, "a_gq", i - 1)
                            vector.scalar_tensor_tensor(
                                w2b[p][:], qps[p][:], rdcol[:, p:p + 1],
                                ebuf[p][:], OP.mult, OP.mult)\
                                .then_inc(sem["a_w2"], 1)
                            W(vector, "a_qk", 2 * i + 2)
                            W(vector, f"a_kst{i % 2}", 16 * (i // 2))
                            vector.tensor_copy(kev[p][:], pa_k[:])\
                                .then_inc(sem["a_kev"], 1)
                    W(vector, "a_gq", NT)
                    vector.tensor_copy(gq_sb[:], pa_gq[:])
                    vector.drain().then_inc(sem["a_gqev"], 1)
                    ph_end(vector)

                def pa_scalar(scalar):
                    ph_start(scalar)
                    for j in range(NT):
                        p = j % 2
                        W(scalar, "a_qk", 2 * j + 1)
                        W(scalar, "a_w2", j - 1)
                        scalar.activation(ebuf[p][:], qps[p][:], AF.Exp,
                                          scale=SCALE,
                                          accum_out=dcol[:, p:p + 1])
                        scalar.drain().then_inc(sem["a_exp"], 1)
                    ph_end(scalar)

                def _mm_block(tensor, psum, pb, whi, wlo, j0):
                    inst = None
                    for ip, (lh, rh) in enumerate(passes):
                        w = [whi, wlo][rh]
                        for kc in range(DC):
                            lhsT = pb[:, kc * 256 + lh * 128:
                                      kc * 256 + lh * 128 + 128]
                            for nch in range(2):
                                inst = tensor.matmul(
                                    psum[:, nch * 512:nch * 512 + 512],
                                    lhsT,
                                    w[:, kc * D + nch * 512:
                                      kc * D + nch * 512 + 512],
                                    start=(ip == 0 and kc == 0),
                                    stop=(ip == 2 and kc == DC - 1))
                    return inst

                def _pa_gq_mm(tensor, j):
                    W(tensor, "a_w2", j + 1)
                    rhs = w2b[j % 2][:]
                    inst = None
                    for nch in range(2):
                        inst = tensor.matmul(
                            pa_gq[:, nch * 512:nch * 512 + 512],
                            ones_r[:, 0:1], rhs[:, nch * 512:nch * 512 + 512],
                            start=(j == 0), stop=(j == NT - 1))
                    inst.then_inc(sem["a_gq"], 1)

                def pa_tensor(tensor):
                    ph_start(tensor)
                    for j in range(NT):
                        W(tensor, "a_split", 2 * (j + 1))
                        W(tensor, "a_w2", j - 1)
                        pb = pairb[j % 2][:]
                        _mm_block(tensor, qps[j % 2][:], pb, wq_hi, wq_lo, j)\
                            .then_inc(sem["a_qk"], 1)
                        W(tensor, "a_kev", j)
                        _mm_block(tensor, pa_k[:], pb, wk_hi, wk_lo, j)\
                            .then_inc(sem["a_qk"], 1)
                        if j >= 1:
                            _pa_gq_mm(tensor, j - 1)
                    _pa_gq_mm(tensor, NT - 1)
                    ph_end(tensor)

                def pa_gpsimd(gpsimd):
                    ph_start(gpsimd)
                    for j in range(NT):
                        W(gpsimd, "a_split", 2 * (j + 1))
                        gpsimd.dma_start(out=xtp_s[j], in_=pairb[j % 2][:])\
                            .then_inc(sem[f"a_pst{j % 2}"], 16)
                        W(gpsimd, "a_kev", j + 1)
                        gpsimd.dma_start(out=k_s[j * 128:(j + 1) * 128, :],
                                         in_=kev[j % 2][:])\
                            .then_inc(sem[f"a_kst{j % 2}"], 16)
                    W(gpsimd, "a_pst0", 16 * (NT // 2))
                    W(gpsimd, "a_pst1", 16 * (NT // 2))
                    W(gpsimd, "a_kst0", 16 * (NT // 2))
                    W(gpsimd, "a_kst1", 16 * (NT // 2))
                    ph_end(gpsimd)

                blk.sync(pa_sync)
                blk.vector(pa_vector)
                blk.scalar(pa_scalar)
                blk.tensor(pa_tensor)
                blk.gpsimd(pa_gpsimd)
            bar_k[0] += 1

        with ExitStack() as stw:
            wsum_hi = stw.enter_context(
                nc.sbuf_tensor("wsum_hi", [128, D * DC], BF16))
            wsum_lo = stw.enter_context(
                nc.sbuf_tensor("wsum_lo", [128, D * DC], BF16))

            with ExitStack() as st1:
                kbuf = [st1.enter_context(nc.sbuf_tensor(f"kbuf{i}", [128, D], F32))
                        for i in range(2)]
                pbuf = [st1.enter_context(nc.sbuf_tensor(f"pbuf{i}", [128, D], F32))
                        for i in range(2)]
                ebuf2 = [st1.enter_context(nc.sbuf_tensor(f"ebuf2{i}", [128, D], F32))
                         for i in range(2)]
                w2b2 = [st1.enter_context(nc.sbuf_tensor(f"w2b2{i}", [128, D], F32R))
                        for i in range(2)]
                dcol2 = st1.enter_context(nc.sbuf_tensor("dcol2", [128, 2], F32))
                rdcol2 = st1.enter_context(nc.sbuf_tensor("rdcol2", [128, 2], F32))
                gq_b = st1.enter_context(nc.sbuf_tensor("gq_b", [128, D], F32))
                gkt_sb = st1.enter_context(nc.sbuf_tensor("gkt_sb", [128, DC], F32))
                wvts_hi = st1.enter_context(
                    nc.sbuf_tensor("wvts_hi", [128, D * DC], BF16))
                wvts_lo = st1.enter_context(
                    nc.sbuf_tensor("wvts_lo", [128, D * DC], BF16))
                wr_hi = st1.enter_context(
                    nc.sbuf_tensor("wr_hi", [128, D * DC], BF16))
                wr_lo = st1.enter_context(
                    nc.sbuf_tensor("wr_lo", [128, D * DC], BF16))
                wraw = [st1.enter_context(
                    nc.sbuf_tensor(f"wraw{i}", [128, D], F32)) for i in range(2)]
                wsf = st1.enter_context(nc.sbuf_tensor("wsf", [128, D], F32))

                with (
                    nc.psum_tensor("pb_gkt", [128, 2 * DC], F32) as pb_gkt,
                    nc.psum_tensor("pb_wc0", [128, D], F32) as pb_wc0,
                    nc.psum_tensor("pb_wc1", [128, D], F32) as pb_wc1,
                    nc.Block("pb") as blk,
                ):
                    wcps = [pb_wc0, pb_wc1]

                    def pb_sync(sync):
                        ph_start(sync)
                        for j in range(NT):
                            W(sync, "b_p", j - 1)
                            sync.dma_start(
                                out=kbuf[j % 2][:],
                                in_=k_s[j * 128:(j + 1) * 128, :],
                            ).then_inc(sem[f"b_ldk{j % 2}"], 16)
                        for i in range(2 * DC):
                            W(sync, "b_wsplit", 2 * (i - 1))
                            src = wvT_d if i < DC else wr_d
                            r = (i % DC) * 128
                            sync.dma_start(out=wraw[i % 2][:],
                                           in_=src[r:r + 128, :])\
                                .then_inc(sem[f"b_wld{i % 2}"], 16)
                        for dc in range(DC):
                            W(sync, "b_wsplit", 4 * DC)
                            W(sync, "b_wsum", dc - 1)
                            sync.dma_start(out=wraw[dc % 2][:],
                                           in_=wq_d[dc * 128:(dc + 1) * 128, :])\
                                .then_inc(sem[f"b_wld{dc % 2}"], 16)
                        ph_end(sync)

                    def pb_vector(vector):
                        ph_start(vector)
                        W(vector, "b_gqb", 16 * 8)
                        for jj in range(NT + 1):
                            if jj < NT:
                                j = jj
                                W(vector, f"b_ldk{j % 2}", 16 * (j // 2 + 1))
                                W(vector, "b_e", j - 1)
                                vector.tensor_tensor(pbuf[j % 2][:],
                                                     kbuf[j % 2][:],
                                                     gq_b[:], OP.mult)\
                                    .then_inc(sem["b_p"], 1)
                            if jj >= 1:
                                i = jj - 1
                                p = i % 2
                                W(vector, "b_e", i + 1)
                                vector.reciprocal(rdcol2[:, p:p + 1],
                                                  dcol2[:, p:p + 1])
                                vector.drain()
                                W(vector, "b_gk", i - 1)
                                vector.scalar_tensor_tensor(
                                    w2b2[p][:], pbuf[p][:],
                                    rdcol2[:, p:p + 1], ebuf2[p][:],
                                    OP.mult, OP.mult)\
                                    .then_inc(sem["b_w2"], 1)
                        W(vector, "b_gk", NT)
                        vector.tensor_copy(
                            gkt_sb[:],
                            pb_gkt[:].rearrange("p (c two) -> p c two",
                                                two=2)[:, :, 0:1])
                        vector.drain().then_inc(sem["b_gkt"], 1)
                        for i in range(2 * DC):
                            W(vector, f"b_wld{i % 2}", 16 * (i // 2 + 1))
                            c = (i % DC) * D
                            if i < DC:
                                vector.tensor_scalar(
                                    wraw[i % 2][:], wraw[i % 2][:],
                                    gkt_sb[:, i:i + 1], None, OP.mult)
                                vector.tensor_copy(wvts_hi[:, c:c + D],
                                                   wraw[i % 2][:])
                                vector.tensor_sub(wvts_lo[:, c:c + D],
                                                  wraw[i % 2][:],
                                                  wvts_hi[:, c:c + D])\
                                    .then_inc(sem["b_wsplit"], 2)
                            else:
                                vector.tensor_copy(wr_hi[:, c:c + D],
                                                   wraw[i % 2][:])
                                vector.tensor_sub(wr_lo[:, c:c + D],
                                                  wraw[i % 2][:],
                                                  wr_hi[:, c:c + D])\
                                    .then_inc(sem["b_wsplit"], 2)
                        for dc in range(DC):
                            W(vector, "b_wc", dc + 1)
                            W(vector, f"b_wld{dc % 2}",
                              16 * ((2 * DC + dc) // 2 + 1))
                            vector.tensor_tensor(wsf[:], wcps[dc % 2][:],
                                                 wraw[dc % 2][:], OP.add)
                            c = dc * D
                            vector.tensor_copy(wsum_hi[:, c:c + D], wsf[:])
                            vector.tensor_sub(wsum_lo[:, c:c + D], wsf[:],
                                              wsum_hi[:, c:c + D])\
                                .then_inc(sem["b_wsum"], 1)
                        ph_end(vector)

                    def pb_scalar(scalar):
                        ph_start(scalar)
                        for j in range(NT):
                            p = j % 2
                            W(scalar, "b_p", j + 1)
                            W(scalar, "b_w2", j - 1)
                            scalar.activation(
                                ebuf2[p][:], pbuf[p][:], AF.Exp, scale=SCALE,
                                accum_out=dcol2[:, p:p + 1])
                            scalar.drain().then_inc(sem["b_e"], 1)
                        ph_end(scalar)

                    def pb_tensor(tensor):
                        ph_start(tensor)
                        for j in range(NT):
                            W(tensor, "b_w2", j + 1)
                            rhs = w2b2[j % 2][:]
                            inst = None
                            for c in range(DC):
                                inst = tensor.matmul(
                                    pb_gkt[:, 2 * c:2 * c + 2],
                                    rhs[:, c * 128:(c + 1) * 128],
                                    ones_r[:, 0:2],
                                    start=(j == 0 and c == 0),
                                    stop=(j == NT - 1 and c == DC - 1))
                            inst.then_inc(sem["b_gk"], 1)
                        for dc in range(DC):
                            W(tensor, "b_wsplit", 4 * DC)
                            W(tensor, "b_wsum", dc - 1)
                            inst = None
                            for ip, (lh, rh) in enumerate(
                                    [(0, 0), (1, 0), (0, 1)]):
                                lhs_t = [wvts_hi, wvts_lo][lh]
                                rhs_t = [wr_hi, wr_lo][rh]
                                for ec in range(DC):
                                    lhsT = lhs_t[:, ec * D + dc * 128:
                                                 ec * D + dc * 128 + 128]
                                    for nch in range(2):
                                        inst = tensor.matmul(
                                            wcps[dc % 2][:, nch * 512:
                                                         nch * 512 + 512],
                                            lhsT,
                                            rhs_t[:, ec * D + nch * 512:
                                                  ec * D + nch * 512 + 512],
                                            start=(ip == 0 and ec == 0),
                                            stop=(ip == 2 and ec == DC - 1))
                            inst.then_inc(sem["b_wc"], 1)
                        ph_end(tensor)

                    def pb_gpsimd(gpsimd):
                        ph_start(gpsimd)
                        W(gpsimd, "a_gqev", 1)
                        gpsimd.dma_start(out=gq_b[0:1, :], in_=gq_sb[:])\
                            .then_inc(sem["b_gqb"], 16)
                        for k in range(6):
                            w = 1 << k
                            W(gpsimd, "b_gqb", 16 * (k + 1))
                            gpsimd.dma_start(out=gq_b[w:2 * w, :],
                                             in_=gq_b[0:w, :])\
                                .then_inc(sem["b_gqb"], 16)
                        W(gpsimd, "b_gqb", 16 * 7)
                        gpsimd.dma_start(out=gq_b[64:128, :], in_=gq_b[0:64, :])\
                            .then_inc(sem["b_gqb"], 16)
                        W(gpsimd, "b_gqb", 16 * 8)
                        ph_end(gpsimd)

                    blk.sync(pb_sync)
                    blk.vector(pb_vector)
                    blk.scalar(pb_scalar)
                    blk.tensor(pb_tensor)
                    blk.gpsimd(pb_gpsimd)
                bar_k[0] += 1

            with ExitStack() as st2:
                xtpb = [st2.enter_context(
                    nc.sbuf_tensor(f"xtpb{i}", [128, 2048], BF16))
                    for i in range(2)]
                xbuf = [st2.enter_context(
                    nc.sbuf_tensor(f"xbuf{i}", [128, D], F32)) for i in range(2)]
                ybuf = [st2.enter_context(
                    nc.sbuf_tensor(f"ybuf{i}", [128, D], F32)) for i in range(2)]
                x1b = [st2.enter_context(
                    nc.sbuf_tensor(f"x1b{i}", [128, D], F32)) for i in range(2)]
                x1hb = [st2.enter_context(
                    nc.sbuf_tensor(f"x1hb{i}", [128, D], BF16)) for i in range(2)]
                x1lb = [st2.enter_context(
                    nc.sbuf_tensor(f"x1lb{i}", [128, D], BF16)) for i in range(2)]
                s1a = st2.enter_context(nc.sbuf_tensor("s1a", [128, NT], F32))
                s2a = st2.enter_context(nc.sbuf_tensor("s2a", [128, NT], F32))
                mucol = st2.enter_context(nc.sbuf_tensor("mucol", [128, 2], F32))
                vcol = st2.enter_context(nc.sbuf_tensor("vcol", [128, 2], F32))
                s0col = st2.enter_context(nc.sbuf_tensor("s0col", [128, 2], F32))
                tcol = st2.enter_context(nc.sbuf_tensor("tcol", [128, 2], F32))
                rcol = st2.enter_context(nc.sbuf_tensor("rcol", [128, 2], F32))
                nmcol = st2.enter_context(nc.sbuf_tensor("nmcol", [128, 2], F32))
                junk = st2.enter_context(nc.sbuf_tensor("junk1", [128, D], BF16))

                with (
                    nc.psum_tensor("c1_a0", [128, D], F32) as c1_a0,
                    nc.psum_tensor("c1_a1", [128, D], F32) as c1_a1,
                    nc.Block("pc1") as blk,
                ):
                    aps = [c1_a0, c1_a1]
                    passes = [(0, 0), (1, 0), (0, 1)]

                    def c1_sync(sync):
                        ph_start(sync)
                        for j in range(NT):
                            W(sync, "c_a", j - 1)
                            sync.dma_start(out=xtpb[j % 2][:], in_=xtp_s[j])\
                                .then_inc(sem[f"c_ldp{j % 2}"], 16)
                            W(sync, "c_y", j - 1)
                            sync.dma_start(out=xbuf[j % 2][:],
                                           in_=x_d[j * 128:(j + 1) * 128, :])\
                                .then_inc(sem[f"c_ldx{j % 2}"], 16)
                        ph_end(sync)

                    def c1_tensor(tensor):
                        ph_start(tensor)
                        for j in range(NT):
                            W(tensor, f"c_ldp{j % 2}", 16 * (j // 2 + 1))
                            W(tensor, "c_y", j - 1)
                            pb = xtpb[j % 2][:]
                            inst = None
                            for ip, (lh, rh) in enumerate(passes):
                                w = [wsum_hi, wsum_lo][rh]
                                for kc in range(DC):
                                    lhsT = pb[:, kc * 256 + lh * 128:
                                              kc * 256 + lh * 128 + 128]
                                    for nch in range(2):
                                        inst = tensor.matmul(
                                            aps[j % 2][:, nch * 512:
                                                       nch * 512 + 512],
                                            lhsT,
                                            w[:, kc * D + nch * 512:
                                              kc * D + nch * 512 + 512],
                                            start=(ip == 0 and kc == 0),
                                            stop=(ip == 2 and kc == DC - 1))
                            inst.then_inc(sem["c_a"], 1)
                        ph_end(tensor)

                    def c1_vector(vector):
                        ph_start(vector)
                        for jj in range(NT + 1):
                            if jj < NT:
                                j = jj
                                p = j % 2
                                W(vector, "c_a", j + 1)
                                W(vector, f"c_ldx{j % 2}", 16 * (j // 2 + 1))
                                W(vector, "c_s2", j - 1)
                                vector.scalar_tensor_tensor(
                                    ybuf[p][:], aps[p][:], 0.0, xbuf[p][:],
                                    OP.bypass, OP.add,
                                    accum_out=s1a[:, j:j + 1])\
                                    .then_inc(sem["c_y"], 1)
                            if jj >= 1:
                                i = jj - 1
                                p = i % 2
                                W(vector, "c_s2", i + 1)
                                vector.tensor_scalar(mucol[:, p:p + 1],
                                                     s1a[:, i:i + 1], 1.0 / D,
                                                     None, OP.mult)
                                vector.drain()
                                vector.tensor_tensor(tcol[:, p:p + 1],
                                                     mucol[:, p:p + 1],
                                                     mucol[:, p:p + 1], OP.mult)
                                vector.drain()
                                vector.scalar_tensor_tensor(
                                    vcol[:, p:p + 1], s2a[:, i:i + 1], 1.0 / D,
                                    tcol[:, p:p + 1], OP.mult, OP.subtract)
                                vector.drain()
                                vector.tensor_scalar(vcol[:, p:p + 1],
                                                     vcol[:, p:p + 1], LN_EPS,
                                                     None, OP.add)
                                vector.drain().then_inc(sem["c_v"], 1)
                                W(vector, "c_sq", i + 1)
                                vector.reciprocal(rcol[:, p:p + 1],
                                                  s0col[:, p:p + 1])
                                vector.drain()
                                vector.tensor_tensor(tcol[:, p:p + 1],
                                                     rcol[:, p:p + 1],
                                                     rcol[:, p:p + 1], OP.mult)
                                vector.drain()
                                vector.tensor_tensor(tcol[:, p:p + 1],
                                                     tcol[:, p:p + 1],
                                                     vcol[:, p:p + 1], OP.mult)
                                vector.drain()
                                vector.tensor_scalar(tcol[:, p:p + 1],
                                                     tcol[:, p:p + 1],
                                                     -0.5, 1.5, OP.mult, OP.add)
                                vector.drain()
                                vector.tensor_tensor(rcol[:, p:p + 1],
                                                     rcol[:, p:p + 1],
                                                     tcol[:, p:p + 1], OP.mult)
                                vector.drain()
                                vector.tensor_tensor(nmcol[:, p:p + 1],
                                                     mucol[:, p:p + 1],
                                                     rcol[:, p:p + 1], OP.mult)
                                vector.drain()
                                vector.tensor_scalar(nmcol[:, p:p + 1],
                                                     nmcol[:, p:p + 1], -1.0,
                                                     None, OP.mult)
                                vector.drain()
                                W(vector, f"c_st{i % 2}", 48 * (i // 2))
                                vector.tensor_scalar(x1b[p][:], ybuf[p][:],
                                                     rcol[:, p:p + 1],
                                                     nmcol[:, p:p + 1],
                                                     OP.mult, OP.add)
                                vector.tensor_copy(x1hb[p][:], x1b[p][:])
                                vector.tensor_sub(x1lb[p][:], x1b[p][:],
                                                  x1hb[p][:])\
                                    .then_inc(sem["c_x1"], 1)
                        ph_end(vector)

                    def c1_scalar(scalar):
                        ph_start(scalar)
                        for jj in range(NT + 1):
                            if jj < NT:
                                j = jj
                                W(scalar, "c_y", j + 1)
                                scalar.activation(junk[:], ybuf[j % 2][:],
                                                  AF.Square,
                                                  accum_out=s2a[:, j:j + 1])
                                scalar.drain().then_inc(sem["c_s2"], 1)
                            if jj >= 1:
                                i = jj - 1
                                W(scalar, "c_v", i + 1)
                                scalar.activation(s0col[:, i % 2:i % 2 + 1],
                                                  vcol[:, i % 2:i % 2 + 1],
                                                  AF.Sqrt)
                                scalar.drain().then_inc(sem["c_sq"], 1)
                        ph_end(scalar)

                    def c1_gpsimd(gpsimd):
                        ph_start(gpsimd)
                        for j in range(NT):
                            p = j % 2
                            W(gpsimd, "c_x1", j + 1)
                            r = slice(j * 128, (j + 1) * 128)
                            gpsimd.dma_start(out=x1_s[r, :], in_=x1b[p][:])\
                                .then_inc(sem[f"c_st{j % 2}"], 16)
                            gpsimd.dma_start(out=x1h_s[r, :], in_=x1hb[p][:])\
                                .then_inc(sem[f"c_st{j % 2}"], 16)
                            gpsimd.dma_start(out=x1l_s[r, :], in_=x1lb[p][:])\
                                .then_inc(sem[f"c_st{j % 2}"], 16)
                        W(gpsimd, "c_st0", 48 * (NT // 2))
                        W(gpsimd, "c_st1", 48 * (NT // 2))
                        ph_end(gpsimd)

                    blk.sync(c1_sync)
                    blk.tensor(c1_tensor)
                    blk.vector(c1_vector)
                    blk.scalar(c1_scalar)
                    blk.gpsimd(c1_gpsimd)
                bar_k[0] += 1

        with ExitStack() as st3:
            wfc_sb = st3.enter_context(
                nc.sbuf_tensor("wfc_sb", [128, DC * H], BF16))
            grpa = [st3.enter_context(
                nc.sbuf_tensor(f"grpa{i}", [128, DC * 2 * 512], BF16))
                for i in range(2)]
            gbuf = [st3.enter_context(
                nc.sbuf_tensor(f"gbuf{i}", [128, H], F32)) for i in range(2)]
            gpf = st3.enter_context(nc.sbuf_tensor("gpf", [128, H], F32))
            gph = [st3.enter_context(
                nc.sbuf_tensor(f"gph{i}", [128, H], BF16)) for i in range(2)]
            gpl = [st3.enter_context(
                nc.sbuf_tensor(f"gpl{i}", [128, H], BF16)) for i in range(2)]
            lnwb = st3.enter_context(nc.sbuf_tensor("lnwb", [128, H], F32))
            s1p = [st3.enter_context(
                nc.sbuf_tensor(f"s1p{i}", [128, 8], F32)) for i in range(2)]
            gtb = [st3.enter_context(
                nc.sbuf_tensor(f"gtb{i}", [128, 1024], BF16)) for i in range(2)]

            with (
                nc.psum_tensor("d_h0", [128, 1024], F32) as d_h0,
                nc.psum_tensor("d_h1", [128, 1024], F32) as d_h1,
                nc.psum_tensor("tr0", [128, 1024], BF16) as tr0,
                nc.psum_tensor("tr1", [128, 1024], BF16) as tr1,
                nc.Block("pc2a") as blk,
            ):
                d_hq = [d_h0, d_h1]
                trq = [tr0, tr1]
                def _xload_a(eng, g, hf):
                    rows = slice(g * 512, (g + 1) * 512)
                    src = (x1h_s, x1l_s)[hf]
                    sm = (f"d_ldT{g % 2}", f"d_ldT2{g % 2}")[hf]
                    for c in range(DC):
                        col = (c * 2 + hf) * 512
                        eng.dma_start(
                            out=grpa[g % 2][:, col:col + 512],
                            in_=src[rows, c * 128:(c + 1) * 128],
                            transpose=True,
                        ).then_inc(sem[sm], 16)

                def d_sync(sync):
                    ph_start(sync)
                    for kc in range(0, DC, 2):
                        sync.dma_start(out=wfc_sb[:, kc * H:(kc + 1) * H],
                                       in_=wfcq_s[kc])\
                            .then_inc(sem["d_wfc"], 16)
                    sync.dma_start(out=lnwb[0:1, :], in_=lnw_row_d[:, :])\
                        .then_inc(sem["d_lnw"], 16)
                    for g in range(NG_A):
                        W(sync, "d_hd", 16 * (g - 1))
                        _xload_a(sync, g, 0)
                        _xload_a(sync, g, 1)
                    ph_end(sync)

                def d_gpsimd(gpsimd):
                    ph_start(gpsimd)
                    for kc in range(1, DC, 2):
                        gpsimd.dma_start(out=wfc_sb[:, kc * H:(kc + 1) * H],
                                         in_=wfcq_s[kc])\
                            .then_inc(sem["d_wfcG"], 16)
                    W(gpsimd, "d_lnw", 16)
                    for k in range(7):
                        w = 1 << k
                        if k > 0:
                            W(gpsimd, "d_gst", 16 * k)
                        gpsimd.dma_start(out=lnwb[w:2 * w, :], in_=lnwb[0:w, :])\
                            .then_inc(sem["d_gst"], 16)
                    for j in range(NT):
                        for b in range(8):
                            i = 8 * j + b
                            W(gpsimd, "d_tev", i + 1)
                            gpsimd.dma_start(
                                out=gT_s[j, :, b * 1024:(b + 1) * 1024],
                                in_=gtb[i % 2][:])\
                                .then_inc(sem[f"d_st{i % 2}"], 16)
                    W(gpsimd, "d_st0", 16 * (8 * NT // 2))
                    W(gpsimd, "d_st1", 16 * (8 * NT // 2))
                    W(gpsimd, "d_gst", 16 * 7)
                    ph_end(gpsimd)

                def d_tensor(tensor):
                    ph_start(tensor)
                    for j in range(NT):
                        g, m = j // 4, j % 4
                        W(tensor, f"d_ldT{g % 2}", 16 * 8 * (g // 2 + 1))
                        W(tensor, f"d_ldT2{g % 2}", 16 * 8 * (g // 2 + 1))
                        if j == 0:
                            W(tensor, "d_wfc", 16 * 4)
                            W(tensor, "d_wfcG", 16 * 4)
                        def _trbatch(tj, b):
                            ib = 8 * tj + b
                            if b == 0:
                                W(tensor, "d_gp", tj + 1)
                            W(tensor, "d_tev", ib - 1)
                            inst = None
                            for sl in range(8):
                                blkidx = b * 8 + sl
                                jc, hf = blkidx // 2, blkidx % 2
                                src = (gph, gpl)[hf][tj % 2][
                                    :, jc * 128:(jc + 1) * 128]
                                inst = tensor.transpose(
                                    trq[ib % 2][:, sl * 128:(sl + 1) * 128],
                                    src, id_sb[:])
                            inst.then_inc(sem["d_trb"], 1)

                        for q in range(4):
                            iq = 4 * j + q
                            W(tensor, "d_gelu", iq - 1)
                            ps = d_hq[iq % 2]
                            inst = None
                            for kc in range(DC):
                                for hf in range(2):
                                    lhsT = grpa[g % 2][
                                        :, (kc * 2 + hf) * 512 + m * 128:
                                           (kc * 2 + hf) * 512 + m * 128 + 128]
                                    for nch in range(2):
                                        inst = tensor.matmul(
                                            ps[:, nch * 512:nch * 512 + 512],
                                            lhsT,
                                            wfc_sb[:, kc * H + q * 1024 +
                                                   nch * 512:
                                                   kc * H + q * 1024 +
                                                   nch * 512 + 512],
                                            start=(kc == 0 and hf == 0),
                                            stop=(kc == DC - 1 and hf == 1))
                            inst.then_inc(sem["d_hd"], 1)
                            if q == 0:
                                if j >= 2:
                                    _trbatch(j - 2, 6)
                                    _trbatch(j - 2, 7)
                            elif j >= 1:
                                _trbatch(j - 1, 2 * (q - 1))
                                _trbatch(j - 1, 2 * q - 1)
                    if NT >= 2:
                        _trbatch(NT - 2, 6)
                        _trbatch(NT - 2, 7)
                    for b in range(8):
                        _trbatch(NT - 1, b)
                    ph_end(tensor)

                def _tevac1(scalar, tj, b):
                    ib = 8 * tj + b
                    W(scalar, "d_trb", ib + 1)
                    W(scalar, f"d_st{ib % 2}", 16 * (ib // 2))
                    scalar.activation(gtb[ib % 2][:], trq[ib % 2][:],
                                      AF.Copy)\
                        .then_inc(sem["d_tev"], 1)

                def d_scalar(scalar):
                    ph_start(scalar)
                    for j in range(NT):
                        p = j % 2
                        for q in range(4):
                            iq = 4 * j + q
                            W(scalar, "d_hd", iq + 1)
                            if q == 0:
                                W(scalar, "d_gp", j - 1)
                            scalar.activation(
                                gbuf[p][:, q * 1024:(q + 1) * 1024],
                                d_hq[iq % 2][:],
                                AF.Gelu,
                                accum_out=s1p[p][:, q:q + 1])\
                                .then_inc(sem["d_gelu"], 1)
                            if q == 0:
                                if j >= 2:
                                    _tevac1(scalar, j - 2, 6)
                                    _tevac1(scalar, j - 2, 7)
                            elif j >= 1:
                                _tevac1(scalar, j - 1, 2 * (q - 1))
                                _tevac1(scalar, j - 1, 2 * q - 1)
                        scalar.drain().then_inc(sem["d_acc"], 1)
                    if NT >= 2:
                        _tevac1(scalar, NT - 2, 6)
                        _tevac1(scalar, NT - 2, 7)
                    for b in range(8):
                        _tevac1(scalar, NT - 1, b)
                    ph_end(scalar)

                def d_vector(vector):
                    ph_start(vector)
                    W(vector, "d_gst", 16 * 7)
                    for j in range(NT):
                        p = j % 2
                        W(vector, "d_acc", j + 1)
                        vector.reduce_sum(s1_t[:, j:j + 1],
                                          s1p[p][:, 0:4], axis=AX.X)
                        vector.scalar_tensor_tensor(
                            gpf[:], gbuf[p][:], 1.0, gbuf[p][:],
                            OP.bypass, OP.mult,
                            accum_out=s2_t[:, j:j + 1])
                        W(vector, "d_trb", 8 * (j - 1))
                        vector.tensor_tensor(gpf[:], gbuf[p][:], lnwb[:],
                                             OP.mult)
                        vector.tensor_copy(gph[p][:], gpf[:])
                        vector.tensor_sub(gpl[p][:], gpf[:], gph[p][:])\
                            .then_inc(sem["d_gp"], 1)
                    ph_end(vector)

                blk.sync(d_sync)
                blk.gpsimd(d_gpsimd)
                blk.tensor(d_tensor)
                blk.scalar(d_scalar)
                blk.vector(d_vector)
            bar_k[0] += 1

        with ExitStack() as st4:
            wpj_sb = st4.enter_context(
                nc.sbuf_tensor("wpj_sb", [128, HC * D], BF16))
            grpb = [st4.enter_context(
                nc.sbuf_tensor(f"grpb{i}", [128, HC * 2 * 256], BF16))
                for i in range(2)]
            x1db = [st4.enter_context(
                nc.sbuf_tensor(f"x1db{i}", [128, 2 * D], F32)) for i in range(2)]
            o1b = [st4.enter_context(
                nc.sbuf_tensor(f"o1b{i}", [128, D], F32)) for i in range(2)]
            o2b = st4.enter_context(nc.sbuf_tensor("o2b", [128, D], F32))
            x1cb = st4.enter_context(nc.sbuf_tensor("x1cb", [128, D], F32))
            y2b = [st4.enter_context(
                nc.sbuf_tensor(f"y2b{i}", [128, D], F32)) for i in range(2)]
            outb = [st4.enter_context(
                nc.sbuf_tensor(f"outb{i}", [128, D], F32)) for i in range(2)]
            c1b = st4.enter_context(nc.sbuf_tensor("c1b", [128, D], F32))
            c2b = st4.enter_context(nc.sbuf_tensor("c2b", [128, D], F32))
            r2a = st4.enter_context(nc.sbuf_tensor("r2a", [128, NT], F32))
            nm2a = st4.enter_context(nc.sbuf_tensor("nm2a", [128, NT], F32))
            tmpa = st4.enter_context(nc.sbuf_tensor("tmpa", [128, NT], F32))
            vea = st4.enter_context(nc.sbuf_tensor("vea", [128, NT], F32))
            s13 = st4.enter_context(nc.sbuf_tensor("s13", [128, 2], F32))
            s23 = st4.enter_context(nc.sbuf_tensor("s23", [128, 2], F32))
            mu3 = st4.enter_context(nc.sbuf_tensor("mu3", [128, 2], F32))
            v3 = st4.enter_context(nc.sbuf_tensor("v3", [128, 2], F32))
            s03 = st4.enter_context(nc.sbuf_tensor("s03", [128, 2], F32))
            t3 = st4.enter_context(nc.sbuf_tensor("t3", [128, 2], F32))
            r3 = st4.enter_context(nc.sbuf_tensor("r3", [128, 2], F32))
            nm3 = st4.enter_context(nc.sbuf_tensor("nm3", [128, 2], F32))
            junk = st4.enter_context(nc.sbuf_tensor("junk3", [128, D], BF16))

            with (
                nc.psum_tensor("e_o0", [128, D], F32) as e_o0,
                nc.psum_tensor("e_o1p", [128, D], F32) as e_o1p,
                nc.Block("pc2b") as blk,
            ):
                ops_ = [e_o0, e_o1p]

                def _gload_b(eng, g):
                    gv = grpb[g % 2][:].rearrange(
                        "p (c mm tt) -> p c mm tt", mm=2, tt=128)
                    for m in range(2):
                        eng.dma_start(
                            out=gv[:, :, m, :],
                            in_=gT_s[2 * g + m],
                        ).then_inc(sem[f"e_ldT{g % 2}"], 16)

                def e_sync(sync):
                    ph_start(sync)
                    for jc in range(0, HC, 2):
                        sync.dma_start(out=wpj_sb[:, jc * D:(jc + 1) * D],
                                       in_=wprojq_s[jc])\
                            .then_inc(sem["e_wpj"], 16)
                    for g in range(NG_B):
                        W(sync, "e_mm", 2 * (g - 1))
                        _gload_b(sync, g)
                        W(sync, "e_y2", 2 * (g - 1))
                        sync.dma_start(
                            out=x1db[g % 2][:].rearrange("p (a d) -> p a d", a=2),
                            in_=x1_s[g * 256:(g + 1) * 256, :].rearrange(
                                "(a p) d -> p a d", p=128),
                        ).then_inc(sem[f"e_ldx{g % 2}"], 16)
                    ph_end(sync)

                def e_vector(vector):
                    ph_start(vector)
                    vector.tensor_scalar(tmpa[:], s1_t[:], 1.0 / H, None,
                                         OP.mult)
                    vector.drain()
                    vector.tensor_tensor(vea[:], tmpa[:], tmpa[:], OP.mult)
                    vector.drain()
                    vector.scalar_tensor_tensor(vea[:], s2_t[:], 1.0 / H,
                                                vea[:], OP.mult, OP.subtract)
                    vector.drain()
                    vector.tensor_scalar(vea[:], vea[:], LN_EPS, None, OP.add)
                    vector.drain().then_inc(sem["e_r2"], 1)
                    W(vector, "e_r2", 2)
                    vector.reciprocal(r2a[:], nm2a[:])
                    vector.drain()
                    vector.tensor_tensor(nm2a[:], r2a[:], r2a[:], OP.mult)
                    vector.drain()
                    vector.tensor_tensor(nm2a[:], nm2a[:], vea[:], OP.mult)
                    vector.drain()
                    vector.tensor_scalar(nm2a[:], nm2a[:], -0.5, 1.5,
                                         OP.mult, OP.add)
                    vector.drain()
                    vector.tensor_tensor(r2a[:], r2a[:], nm2a[:], OP.mult)
                    vector.drain()
                    vector.tensor_tensor(nm2a[:], tmpa[:], r2a[:], OP.mult)
                    vector.drain()
                    vector.tensor_scalar(nm2a[:], nm2a[:], -1.0, None, OP.mult)
                    vector.drain().then_inc(sem["e_r2"], 1)
                    W(vector, "e_c12b", 16 * 16)
                    for jj in range(NT + 2):
                        if jj >= 2:
                            i2 = jj - 2
                            p = i2 % 2
                            W(vector, "e_sq3", i2 + 1)
                            vector.reciprocal(r3[:, p:p + 1], s03[:, p:p + 1])
                            vector.drain()
                            vector.tensor_tensor(t3[:, p:p + 1], r3[:, p:p + 1],
                                                 r3[:, p:p + 1], OP.mult)
                            vector.drain()
                            vector.tensor_tensor(t3[:, p:p + 1], t3[:, p:p + 1],
                                                 v3[:, p:p + 1], OP.mult)
                            vector.drain()
                            vector.tensor_scalar(t3[:, p:p + 1], t3[:, p:p + 1],
                                                 -0.5, 1.5, OP.mult, OP.add)
                            vector.drain()
                            vector.tensor_tensor(r3[:, p:p + 1], r3[:, p:p + 1],
                                                 t3[:, p:p + 1], OP.mult)
                            vector.drain()
                            vector.tensor_tensor(nm3[:, p:p + 1],
                                                 mu3[:, p:p + 1],
                                                 r3[:, p:p + 1], OP.mult)
                            vector.drain()
                            vector.tensor_scalar(nm3[:, p:p + 1],
                                                 nm3[:, p:p + 1], -1.0,
                                                 None, OP.mult)
                            vector.drain()
                            W(vector, f"e_st{i2 % 2}", 16 * (i2 // 2))
                            vector.tensor_scalar(outb[p][:], y2b[p][:],
                                                 r3[:, p:p + 1],
                                                 nm3[:, p:p + 1],
                                                 OP.mult, OP.add)\
                                .then_inc(sem["e_out"], 1)
                        if 1 <= jj <= NT:
                            i1 = jj - 1
                            p = i1 % 2
                            W(vector, "e_s23", i1 + 1)
                            W(vector, "e_sq3", i1 - 1)
                            vector.tensor_scalar(mu3[:, p:p + 1],
                                                 s13[:, p:p + 1], 1.0 / D,
                                                 None, OP.mult)
                            vector.drain()
                            vector.tensor_tensor(t3[:, p:p + 1],
                                                 mu3[:, p:p + 1],
                                                 mu3[:, p:p + 1], OP.mult)
                            vector.drain()
                            vector.scalar_tensor_tensor(
                                v3[:, p:p + 1], s23[:, p:p + 1], 1.0 / D,
                                t3[:, p:p + 1], OP.mult, OP.subtract)
                            vector.drain()
                            vector.tensor_scalar(v3[:, p:p + 1], v3[:, p:p + 1],
                                                 LN_EPS, None, OP.add)
                            vector.drain().then_inc(sem["e_v3"], 1)
                        if jj < NT:
                            j = jj
                            p = j % 2
                            g = j // 2
                            W(vector, "e_o1", j + 1)
                            W(vector, f"e_ldx{g % 2}", 16 * (g // 2 + 1))
                            vector.scalar_tensor_tensor(
                                o2b[:], c1b[:], nm2a[:, j:j + 1], o1b[p][:],
                                OP.mult, OP.add)
                            vector.tensor_tensor(
                                x1cb[:], x1db[g % 2][:, (j % 2) * D:
                                                     (j % 2) * D + D],
                                c2b[:], OP.add)
                            W(vector, "e_s23", j - 1)
                            vector.scalar_tensor_tensor(
                                y2b[p][:], o2b[:], 0.0, x1cb[:],
                                OP.bypass, OP.add,
                                accum_out=s13[:, p:p + 1])\
                                .then_inc(sem["e_y2"], 1)
                    ph_end(vector)

                def e_scalar(scalar):
                    ph_start(scalar)
                    W(scalar, "e_r2", 1)
                    scalar.activation(nm2a[:], vea[:], AF.Sqrt)
                    scalar.drain().then_inc(sem["e_r2"], 1)
                    for jj in range(NT + 2):
                        if jj < NT:
                            j = jj
                            p = j % 2
                            W(scalar, "e_mm", j + 1)
                            W(scalar, "e_r2", 3)
                            W(scalar, "e_y2", j - 1)
                            scalar.activation(o1b[p][:], ops_[p][:], AF.Copy,
                                              scale=r2a[:, j:j + 1])\
                                .then_inc(sem["e_o1"], 1)
                        if 1 <= jj <= NT:
                            i = jj - 1
                            W(scalar, "e_y2", i + 1)
                            W(scalar, "e_v3", i - 1)
                            scalar.activation(junk[:], y2b[i % 2][:],
                                              AF.Square,
                                              accum_out=s23[:, i % 2:i % 2 + 1])
                            scalar.drain().then_inc(sem["e_s23"], 1)
                        if jj >= 2:
                            i2 = jj - 2
                            W(scalar, "e_v3", i2 + 1)
                            W(scalar, "e_out", i2 - 1)
                            scalar.activation(s03[:, i2 % 2:i2 % 2 + 1],
                                              v3[:, i2 % 2:i2 % 2 + 1],
                                              AF.Sqrt)
                            scalar.drain().then_inc(sem["e_sq3"], 1)
                    ph_end(scalar)

                def e_tensor(tensor):
                    ph_start(tensor)
                    for j in range(NT):
                        g, m = j // 2, j % 2
                        W(tensor, f"e_ldT{g % 2}", 16 * 2 * (g // 2 + 1))
                        W(tensor, "e_o1", j - 1)
                        if j == 0:
                            W(tensor, "e_wpj", 16 * 16)
                            W(tensor, "e_wpjG", 16 * 16)
                        inst = None
                        for jc in range(HC):
                            for hf in range(2):
                                lhsT = grpb[g % 2][
                                    :, (jc * 2 + hf) * 256 + m * 128:
                                       (jc * 2 + hf) * 256 + m * 128 + 128]
                                for nch in range(2):
                                    inst = tensor.matmul(
                                        ops_[j % 2][:, nch * 512:nch * 512 + 512],
                                        lhsT,
                                        wpj_sb[:, jc * D + nch * 512:
                                               jc * D + nch * 512 + 512],
                                        start=(jc == 0 and hf == 0),
                                        stop=(jc == HC - 1 and hf == 1))
                        inst.then_inc(sem["e_mm"], 1)
                    ph_end(tensor)

                def e_gpsimd(gpsimd):
                    ph_start(gpsimd)
                    for jc in range(1, HC, 2):
                        gpsimd.dma_start(out=wpj_sb[:, jc * D:(jc + 1) * D],
                                         in_=wprojq_s[jc])\
                            .then_inc(sem["e_wpjG"], 16)
                    gpsimd.dma_start(out=c1b[0:1, :], in_=c12_sb[0:1, :])\
                        .then_inc(sem["e_c12b"], 16)
                    gpsimd.dma_start(out=c2b[0:1, :], in_=c12_sb[1:2, :])\
                        .then_inc(sem["e_c12b"], 16)
                    for k in range(6):
                        w = 1 << k
                        W(gpsimd, "e_c12b", 16 * 2 * (k + 1))
                        gpsimd.dma_start(out=c1b[w:2 * w, :], in_=c1b[0:w, :])\
                            .then_inc(sem["e_c12b"], 16)
                        gpsimd.dma_start(out=c2b[w:2 * w, :], in_=c2b[0:w, :])\
                            .then_inc(sem["e_c12b"], 16)
                    W(gpsimd, "e_c12b", 16 * 14)
                    gpsimd.dma_start(out=c1b[64:128, :], in_=c1b[0:64, :])\
                        .then_inc(sem["e_c12b"], 16)
                    gpsimd.dma_start(out=c2b[64:128, :], in_=c2b[0:64, :])\
                        .then_inc(sem["e_c12b"], 16)
                    for j in range(NT):
                        W(gpsimd, "e_out", j + 1)
                        gpsimd.dma_start(out=out_d[j * 128:(j + 1) * 128, :],
                                         in_=outb[j % 2][:])\
                            .then_inc(sem[f"e_st{j % 2}"], 16)
                    ndbg = 0
                    if dbg:
                        for dsts in [(dbg_d[0:1, :], gq_sb[:]),
                                     (dbg_d[1:2, :], c12_sb[0:1, :]),
                                     (dbg_d[2:3, :], c12_sb[1:2, :]),
                                     (dbg2_d[:, 0:NT], s1_t[:]),
                                     (dbg2_d[:, NT:2 * NT], s2_t[:]),
                                     (dbg2_d[:, 2 * NT:3 * NT], r2a[:]),
                                     (dbg2_d[:, 3 * NT:4 * NT], nm2a[:])]:
                            gpsimd.dma_start(out=dsts[0], in_=dsts[1])\
                                .then_inc(sem["e_std"], 16)
                            ndbg += 1
                    W(gpsimd, "e_st0", 16 * (NT // 2))
                    W(gpsimd, "e_st1", 16 * (NT // 2))
                    if ndbg:
                        W(gpsimd, "e_std", 16 * ndbg)
                    ph_end(gpsimd)

                blk.sync(e_sync)
                blk.vector(e_vector)
                blk.scalar(e_scalar)
                blk.tensor(e_tensor)
                blk.gpsimd(e_gpsimd)
            bar_k[0] += 1

    return nc


def kernel(L0_attn_x, L0_attn_Wq, L0_attn_Wk, L0_attn_Wv, L0_attn_Wr,
           L0_ffn_W_fc, L0_ffn_ln_w, L0_ffn_ln_b, L0_ffn_W_proj):
    B, S, D_ = L0_attn_x.shape
    assert D_ == D
    x = np.ascontiguousarray(np.asarray(L0_attn_x, dtype=np.float32))
    f32 = lambda a: np.ascontiguousarray(np.asarray(a, dtype=np.float32))
    wq = f32(L0_attn_Wq)
    wk = f32(L0_attn_Wk)
    wvT = np.ascontiguousarray(f32(L0_attn_Wv).T)
    wr = f32(L0_attn_Wr)
    wfcT = np.ascontiguousarray(f32(L0_ffn_W_fc).T)
    wprojT = np.ascontiguousarray(f32(L0_ffn_W_proj).T)
    lnw = f32(L0_ffn_ln_w).reshape(-1)
    lnb = f32(L0_ffn_ln_b).reshape(-1)
    lnw_row = np.ascontiguousarray(lnw.reshape(1, H))
    lnwT = np.ascontiguousarray(lnw.reshape(HC, 128).T)
    lnbT = np.ascontiguousarray(lnb.reshape(HC, 128).T)

    nc = build(S)
    in_maps = []
    for i in range(B):
        in_maps.append({
            "x": np.ascontiguousarray(x[i]),
            "xT": np.ascontiguousarray(x[i].T),
            "wq": wq, "wk": wk, "wvT": wvT, "wr": wr,
            "wfcT": wfcT, "wprojT": wprojT,
            "lnw_row": lnw_row, "lnwT": lnwT, "lnbT": lnbT,
        })
    res = run_bass_kernel_spmd(nc, in_maps, core_ids=list(range(B)))
    out = np.stack([res.results[i]["out"] for i in range(B)], axis=0)
    return out.astype(np.float32)
